# version 04
# baseline (speedup 1.0000x reference)
import contextlib
import numpy as np

import concourse.bass as bass
import concourse.mybir as mybir
from concourse.bass_utils import run_bass_kernel_spmd

F32 = mybir.dt.float32
BF16 = mybir.dt.bfloat16
U32 = mybir.dt.uint32
AF = mybir.ActivationFunctionType
ALU = mybir.AluOpType
AX = mybir.AxisListType

D = 1024
S = 4096
DI = 2048
NST = 16
DTR = 64
HID = 4096
HEADS = 16
DH = 64
LAT = 256
NEXP = 8
SCALE_ATTN = DH ** -0.5
TC = 1024
NMM = 512
NSUB = TC // NMM


_UID = [0]


def sbuf(nc, name, shape, dtype):
    _UID[0] += 1
    return nc.sbuf_tensor(f"{name}_{_UID[0]}", shape, dtype)


def psum(nc, name, shape, dtype=F32):
    _UID[0] += 1
    return nc.psum_tensor(f"{name}_{_UID[0]}", shape, dtype)


class Sem:

    def __init__(self, es, nc, name):
        self.es, self.nc, self.name = es, nc, name
        self.h = es.enter_context(nc.semaphore(name))
        self.n = 0
        self.uid = 0

    def inc(self, instr, k=1):
        instr.then_inc(self.h, k)
        self.n += k
        return self.n

    def rotate(self):
        if self.n < 12000:
            return
        self.uid += 1
        self.h = self.es.enter_context(self.nc.semaphore(f"{self.name}r{self.uid}"))
        self.n = 0


class Ctx:

    def __init__(self, nc, es):
        self.nc = nc
        self.es = es
        self.sW = Sem(es, nc, "sW")
        self.sX = Sem(es, nc, "sX")
        self.sPE = Sem(es, nc, "sPE")
        self.sACT = Sem(es, nc, "sACT")
        self.sDVE = Sem(es, nc, "sDVE")
        self.sPL = Sem(es, nc, "sPL")
        self.guards = {}
        self.ones_f32 = es.enter_context(sbuf(nc, "c_ones_f", [128, 128], F32))
        self.ones_bf = es.enter_context(sbuf(nc, "c_ones_b", [128, 128], BF16))
        self.ident_bf = es.enter_context(sbuf(nc, "c_id_b", [128, 128], BF16))
        nc.gpsimd.memset(self.ones_f32[:], 1.0)
        nc.gpsimd.memset(self.ones_bf[:], 1.0)
        self.cvals = es.enter_context(sbuf(nc, "c_vals", [128, 8], F32))
        for i, v in enumerate([1e-5, -0.5, SCALE_ATTN]):
            nc.gpsimd.memset(self.cvals[:, i:i + 1], v)
            nc.const_aps.register(F32, v, self.cvals[:, i:i + 1])
        ev = self.sPL.inc(nc.gpsimd.affine_select(
            self.ident_bf[:], self.ones_bf[:], [[-1, 128]], ALU.is_equal, 0.0,
            base=0, channel_multiplier=1))
        for eng in (nc.tensor, nc.vector, nc.scalar, nc.sync):
            eng.wait_ge(self.sPL.h, ev)

    def barrier(self):
        nc = self.nc
        for eng in (nc.tensor, nc.vector, nc.scalar, nc.gpsimd, nc.sync):
            for s in (self.sW, self.sX, self.sPE, self.sACT, self.sDVE, self.sPL):
                if s.n:
                    eng.wait_ge(s.h, s.n)
        for s in (self.sW, self.sX, self.sPE, self.sACT, self.sDVE, self.sPL):
            s.rotate()
        self.guards.clear()

    def gwait(self, eng, name):
        g = self.guards.get(name)
        if g is not None:
            eng.wait_ge(g[0], g[1])

    def gset(self, name, sem, ev):
        self.guards[name] = (sem.h, ev)


def pe_mark(c, instr):
    instr.then_inc(c.sPE.h, 1)
    c.sPE.n += 1
    return c.sPE.n


def bcol(t, col, ncols, width):
    return bass.AP(t, col, [[ncols, 128], [0, width]])


def emit_ln(c, x_hbm, xn_hbm, dbg_stats=None):
    nc = c.nc
    with contextlib.ExitStack() as es:
        xt = [es.enter_context(sbuf(nc, f"ln_x{i}", [128, TC], F32)) for i in range(8)]
        xb = [es.enter_context(sbuf(nc, f"ln_b{i}", [128, TC], BF16)) for i in range(8)]
        xq = [es.enter_context(sbuf(nc, f"ln_q{i}", [128, TC], BF16)) for i in range(8)]
        mean_sb = es.enter_context(sbuf(nc, "ln_mean", [1, TC], F32))
        var_sb = es.enter_context(sbuf(nc, "ln_var", [1, TC], F32))
        rstd_sb = es.enter_context(sbuf(nc, "ln_rstd", [1, TC], F32))
        nmr_sb = es.enter_context(sbuf(nc, "ln_nmr", [1, TC], F32))
        t1 = es.enter_context(sbuf(nc, "ln_t1", [128, TC], F32))
        xn = es.enter_context(sbuf(nc, "ln_xn", [128, TC], BF16))
        sum_ps = es.enter_context(psum(nc, "ln_sum_ps", [1, TC], F32))
        ssq_ps = es.enter_context(psum(nc, "ln_ssq_ps", [1, TC], F32))
        brstd_ps = es.enter_context(psum(nc, "ln_brstd_ps", [128, TC], F32))
        bnmr_ps = es.enter_context(psum(nc, "ln_bnmr_ps", [128, TC], F32))
        for ch in range(S // TC):
            t0 = ch * TC
            ev_ld = []
            for d in range(8):
                c.gwait(nc.sync, "ln_xt")
                ev_ld.append(c.sX.inc(nc.sync.dma_start(
                    out=xt[d][:], in_=x_hbm[d * 128:(d + 1) * 128, t0:t0 + TC]), 16))
            ev_sq = []
            for d in range(8):
                nc.scalar.wait_ge(c.sX.h, ev_ld[d])
                c.gwait(nc.scalar, "ln_xq")
                nc.scalar.activation(xb[d][:], xt[d][:], AF.Identity)
                ev_sq.append(c.sACT.inc(nc.scalar.activation(xq[d][:], xt[d][:], AF.Square)))
            c.gwait(nc.tensor, "ln_stats_ps")
            for d in range(8):
                nc.tensor.wait_ge(c.sACT.h, ev_sq[d])
                for n2 in range(NSUB):
                    sl = slice(n2 * NMM, (n2 + 1) * NMM)
                    nc.tensor.matmul(sum_ps[:, sl], c.ones_bf[:, 0:1], xb[d][:, sl],
                                     start=(d == 0), stop=(d == 7))
            for d in range(8):
                for n2 in range(NSUB):
                    sl = slice(n2 * NMM, (n2 + 1) * NMM)
                    mm = nc.tensor.matmul(ssq_ps[:, sl], c.ones_bf[:, 0:1], xq[d][:, sl],
                                          start=(d == 0), stop=(d == 7))
            ev_stats = pe_mark(c, mm)
            c.gset("ln_xq", c.sPE, ev_stats)
            nc.vector.wait_ge(c.sPE.h, ev_stats)
            nc.vector.tensor_scalar(mean_sb[:], sum_ps[:], 1.0 / D, None, ALU.mult)
            nc.vector.tensor_scalar(var_sb[:], ssq_ps[:], 1.0 / D, None, ALU.mult)
            nc.vector.tensor_mul(t1[0:1, :], mean_sb[:], mean_sb[:])
            ev_var = c.sDVE.inc(nc.vector.tensor_sub(var_sb[:], var_sb[:], t1[0:1, :]))
            nc.scalar.wait_ge(c.sDVE.h, ev_var)
            nc.scalar.activation(rstd_sb[:], var_sb[:], AF.Ln, bias=1e-5)
            ev_rstd = c.sACT.inc(nc.scalar.activation(rstd_sb[:], rstd_sb[:], AF.Exp, scale=-0.5))
            nc.vector.wait_ge(c.sACT.h, ev_rstd)
            ev_nmr = c.sDVE.inc(nc.vector.scalar_tensor_tensor(
                nmr_sb[:], mean_sb[:], -1.0, rstd_sb[:], ALU.mult, ALU.mult))
            c.gset("ln_stats_ps", c.sDVE, ev_nmr)
            if dbg_stats is not None:
                nc.sync.wait_ge(c.sDVE.h, ev_nmr)
                c.sX.inc(nc.sync.dma_start(out=dbg_stats[0:1, t0:t0 + TC], in_=mean_sb[:]), 16)
                c.sX.inc(nc.sync.dma_start(out=dbg_stats[1:2, t0:t0 + TC], in_=var_sb[:]), 16)
                c.sX.inc(nc.sync.dma_start(out=dbg_stats[2:3, t0:t0 + TC], in_=rstd_sb[:]), 16)
                c.sX.inc(nc.sync.dma_start(out=dbg_stats[3:4, t0:t0 + TC], in_=nmr_sb[:]), 16)
            c.gwait(nc.tensor, "ln_bcast_ps")
            nc.tensor.wait_ge(c.sDVE.h, ev_nmr)
            for n2 in range(NSUB):
                sl = slice(n2 * NMM, (n2 + 1) * NMM)
                nc.tensor.matmul(brstd_ps[:, sl], c.ones_f32[0:1, :], rstd_sb[:, sl],
                                 start=True, stop=True)
                mm = nc.tensor.matmul(bnmr_ps[:, sl], c.ones_f32[0:1, :], nmr_sb[:, sl],
                                      start=True, stop=True)
            ev_bcast = pe_mark(c, mm)
            nc.vector.wait_ge(c.sPE.h, ev_bcast)
            for d in range(8):
                c.gwait(nc.vector, "ln_xn")
                nc.vector.tensor_mul(t1[:], xt[d][:], brstd_ps[:])
                ev_xn = c.sDVE.inc(nc.vector.tensor_add(xn[:], t1[:], bnmr_ps[:]))
                nc.sync.wait_ge(c.sDVE.h, ev_xn)
                ev_st = c.sX.inc(nc.sync.dma_start(
                    out=xn_hbm[d * 128:(d + 1) * 128, t0:t0 + TC], in_=xn[:]), 16)
                c.gset("ln_xn", c.sX, ev_st)
            c.gset("ln_bcast_ps", c.sDVE, c.sDVE.n)
            c.gset("ln_xt", c.sDVE, c.sDVE.n)
        c.barrier()


def emit_ffn0(c, w, x_hbm, xn_hbm, xout_hbm):
    nc = c.nc
    with contextlib.ExitStack() as es:
        s1t = es.enter_context(sbuf(nc, "f_s1", [128, 32], F32))
        s2t = es.enter_context(sbuf(nc, "f_s2", [128, 32], F32))
        h1st = es.enter_context(sbuf(nc, "f_h1s", [128, 32], F32))
        h2st = es.enter_context(sbuf(nc, "f_h2s", [128, 32], F32))
        w3st = es.enter_context(sbuf(nc, "f_w3s", [128, 8], F32))
        w3h = es.enter_context(sbuf(nc, "f_w3h", [128, 8], F32))
        ev = 0
        for dst, src in ((s1t, w["s1"]), (s2t, w["s2"]), (h1st, w["h1s"]),
                         (h2st, w["h2s"]), (w3st, w["w3s"])):
            ev = c.sX.inc(nc.sync.dma_start(out=dst[:], in_=src), 16)
        nc.vector.wait_ge(c.sX.h, ev)
        ev_w3h = c.sDVE.inc(nc.vector.tensor_scalar(w3h[:], w3st[:], 0.5, None, ALU.mult))
        nc.vector.wait_ge(c.sDVE.h, ev_w3h)
        nc.scalar.wait_ge(c.sX.h, ev)

        xnt = [es.enter_context(sbuf(nc, f"f_xn{i}", [128, TC], BF16)) for i in range(8)]
        hbuf = [es.enter_context(sbuf(nc, f"f_h{i}", [128, TC], BF16)) for i in range(32)]
        mbuf = [es.enter_context(sbuf(nc, f"f_m{i}", [128, TC], BF16)) for i in range(32)]
        sg = es.enter_context(sbuf(nc, "f_sg", [128, TC], BF16))
        xr = es.enter_context(sbuf(nc, "f_xr", [128, TC], F32))
        xo = es.enter_context(sbuf(nc, "f_xo", [128, TC], F32))
        wpan = [es.enter_context(sbuf(nc, f"f_wp{i}", [128, 32 * 128], BF16)) for i in range(2)]
        wg = [es.enter_context(sbuf(nc, f"f_wg{i}", [128, 8 * 128], BF16)) for i in range(4)]
        ps = [es.enter_context(psum(nc, f"f_ps{i}", [128, NMM], F32)) for i in range(8)]

        for ch in range(S // TC):
            t0 = ch * TC
            ev_xn = []
            for d in range(8):
                c.gwait(nc.sync, "f_xnt")
                ev_xn.append(c.sX.inc(nc.sync.dma_start(
                    out=xnt[d][:], in_=xn_hbm[d * 128:(d + 1) * 128, t0:t0 + TC]), 16))

            for m in range(32):
                b1 = wg[m % 2]
                b2 = wg[2 + m % 2]
                c.gwait(nc.sync, f"f_wg{m % 2}")
                ev_p1 = c.sW.inc(nc.sync.dma_start(out=b1[:], in_=w["w1p"][m]), 16)
                c.gwait(nc.sync, f"f_wg{2 + m % 2}")
                ev_p2 = c.sW.inc(nc.sync.dma_start(out=b2[:], in_=w["w2p"][m]), 16)
                nc.tensor.wait_ge(c.sW.h, ev_p2)
                nc.tensor.wait_ge(c.sX.h, ev_xn[7])
                for n2 in range(NSUB):
                    s0, s1_ = 2 * n2, 2 * n2 + 1
                    c.gwait(nc.tensor, f"f_ps{s0}")
                    c.gwait(nc.tensor, f"f_ps{s1_}")
                    for k in range(8):
                        nc.tensor.matmul(ps[s0][:], b1[:, k * 128:(k + 1) * 128],
                                         xnt[k][:, n2 * NMM:(n2 + 1) * NMM],
                                         start=(k == 0), stop=(k == 7))
                    for k in range(8):
                        mm = nc.tensor.matmul(ps[s1_][:], b2[:, k * 128:(k + 1) * 128],
                                              xnt[k][:, n2 * NMM:(n2 + 1) * NMM],
                                              start=(k == 0), stop=(k == 7))
                    ev_mm = pe_mark(c, mm)
                    nc.scalar.wait_ge(c.sPE.h, ev_mm)
                    c.gwait(nc.scalar, "f_sg")
                    ev_a = c.sACT.inc(nc.scalar.activation(
                        sg[:, n2 * NMM:(n2 + 1) * NMM], ps[s0][:], AF.Silu,
                        scale=s1t[:, m:m + 1]))
                    nc.vector.wait_ge(c.sACT.h, ev_a)
                    ev_v = c.sDVE.inc(nc.vector.scalar_tensor_tensor(
                        hbuf[m][:, n2 * NMM:(n2 + 1) * NMM], ps[s1_][:],
                        s2t[:, m:m + 1], sg[:, n2 * NMM:(n2 + 1) * NMM],
                        ALU.mult, ALU.mult))
                    c.gset(f"f_ps{s0}", c.sACT, ev_a)
                    c.gset(f"f_ps{s1_}", c.sDVE, ev_v)
                    c.gset("f_sg", c.sDVE, ev_v)
                c.gset(f"f_wg{m % 2}", c.sPE, c.sPE.n)
                c.gset(f"f_wg{2 + m % 2}", c.sPE, c.sPE.n)
            c.gset("f_xnt", c.sPE, c.sPE.n)
            ev_h_done = (c.sDVE.h, c.sDVE.n)

            def stage_hh(panel_list, src_bufs, dst_bufs, scale_t, ev_src):
                for m in range(32):
                    pb = wpan[m % 2]
                    c.gwait(nc.sync, f"f_wpan{m % 2}")
                    ev_p = c.sW.inc(nc.sync.dma_start(out=pb[:], in_=panel_list[m]), 16)
                    nc.tensor.wait_ge(c.sW.h, ev_p)
                    nc.tensor.wait_ge(ev_src[0], ev_src[1])
                    for n2 in range(NSUB):
                        slot = (m * NSUB + n2) % 8
                        c.gwait(nc.tensor, f"f_ps{slot}")
                        for k in range(32):
                            mm = nc.tensor.matmul(ps[slot][:], pb[:, k * 128:(k + 1) * 128],
                                                  src_bufs[k][:, n2 * NMM:(n2 + 1) * NMM],
                                                  start=(k == 0), stop=(k == 31))
                        ev_mm = pe_mark(c, mm)
                        nc.scalar.wait_ge(c.sPE.h, ev_mm)
                        ev_a = c.sACT.inc(nc.scalar.activation(
                            dst_bufs[m][:, n2 * NMM:(n2 + 1) * NMM], ps[slot][:],
                            AF.Gelu, scale=scale_t[:, m:m + 1]))
                        c.gset(f"f_ps{slot}", c.sACT, ev_a)
                    c.gset(f"f_wpan{m % 2}", c.sPE, c.sPE.n)
                return (c.sACT.h, c.sACT.n)

            ev_m_done = stage_hh(w["h1p"], hbuf, mbuf, h1st, ev_h_done)
            ev_h2_done = stage_hh(w["h2p"], mbuf, hbuf, h2st, ev_m_done)

            for m in range(8):
                pb = wpan[m % 2]
                c.gwait(nc.sync, f"f_wpan{m % 2}")
                ev_p = c.sW.inc(nc.sync.dma_start(out=pb[:], in_=w["w3p"][m]), 16)
                c.gwait(nc.sync, "f_xr")
                ev_xr = c.sX.inc(nc.sync.dma_start(
                    out=xr[:], in_=x_hbm[m * 128:(m + 1) * 128, t0:t0 + TC]), 16)
                nc.tensor.wait_ge(c.sW.h, ev_p)
                nc.tensor.wait_ge(ev_h2_done[0], ev_h2_done[1])
                for n2 in range(NSUB):
                    slot = (m * NSUB + n2) % 8
                    c.gwait(nc.tensor, f"f_ps{slot}")
                    for k in range(32):
                        mm = nc.tensor.matmul(ps[slot][:], pb[:, k * 128:(k + 1) * 128],
                                              hbuf[k][:, n2 * NMM:(n2 + 1) * NMM],
                                              start=(k == 0), stop=(k == 31))
                    ev_mm = pe_mark(c, mm)
                    nc.vector.wait_ge(c.sPE.h, ev_mm)
                    nc.vector.wait_ge(c.sX.h, ev_xr)
                    c.gwait(nc.vector, "f_xo")
                    ev_v = c.sDVE.inc(nc.vector.scalar_tensor_tensor(
                        xo[:, n2 * NMM:(n2 + 1) * NMM], ps[slot][:],
                        w3h[:, m:m + 1], xr[:, n2 * NMM:(n2 + 1) * NMM],
                        ALU.mult, ALU.add))
                    c.gset(f"f_ps{slot}", c.sDVE, ev_v)
                c.gset("f_xr", c.sDVE, c.sDVE.n)
                nc.sync.wait_ge(c.sDVE.h, c.sDVE.n)
                ev_st = c.sX.inc(nc.sync.dma_start(
                    out=xout_hbm[m * 128:(m + 1) * 128, t0:t0 + TC], in_=xo[:]), 16)
                c.gset("f_xo", c.sX, ev_st)
                c.gset(f"f_wpan{m % 2}", c.sPE, c.sPE.n)
        c.barrier()


def emit_mamba(c, w, x_hbm, xn_hbm, xout_hbm, scr, passes=(1, 1, 1)):
    nc = c.nc
    NDT = DI // 128
    with contextlib.ExitStack() as es:
        wx_sb = es.enter_context(sbuf(nc, "ms_wx", [128, 16 * 96], BF16))
        wdt_sb = es.enter_context(sbuf(nc, "ms_wdt", [64, DI], BF16))
        cw = es.enter_context(sbuf(nc, "ms_cw", [128, 16 * 4], F32))
        cb = es.enter_context(sbuf(nc, "ms_cb", [128, 16], F32))
        bdt = es.enter_context(sbuf(nc, "ms_bdt", [128, 16], F32))
        dcol = es.enter_context(sbuf(nc, "mb_d", [128, 16], F32))
        acols = es.enter_context(sbuf(nc, "ms_acols", [128, 256], F32))
        negA = es.enter_context(sbuf(nc, "mb_negA", [128, 256], F32))
        sel = es.enter_context(sbuf(nc, "mb_sel", [32, 32 * 128], F32))
        bc_sb = es.enter_context(sbuf(nc, "mb_bc", [32, S], F32))
        hcarry = [es.enter_context(sbuf(nc, f"mb_hc{i}", [128, 256], F32)) for i in range(1)]
        hcarry = hcarry[0]
        halo = es.enter_context(sbuf(nc, "mb_halo", [128, 16 * 3], F32))
        ev = 0
        ev = c.sW.inc(nc.sync.dma_start(out=wx_sb[:], in_=w["wx"]), 16)
        ev = c.sW.inc(nc.sync.dma_start(out=wdt_sb[:], in_=w["wdt"]), 16)
        ev_w = 0
        for dst, src in ((cw, w["cw"]), (cb, w["cb"]), (bdt, w["bdt"]),
                         (dcol, w["dcol"]), (acols, w["acols"])):
            ev_w = c.sX.inc(nc.sync.dma_start(out=dst[:], in_=src), 16)
        nc.scalar.wait_ge(c.sX.h, ev_w)
        ev_a = c.sACT.inc(nc.scalar.activation(negA[:], acols[:], AF.Exp))
        nc.vector.wait_ge(c.sACT.h, ev_a)
        ev_na = c.sDVE.inc(nc.vector.tensor_scalar(negA[:], negA[:], -1.0, None, ALU.mult))
        nc.scalar.wait_ge(c.sDVE.h, ev_na)
        nc.vector.wait_ge(c.sX.h, ev_w)
        nc.gpsimd.memset(sel[:], 1.0)
        ev_sel = c.sPL.inc(nc.gpsimd.affine_select(
            sel[:], sel[:], [[-1, 32], [0, 128]], ALU.is_equal, 0.0,
            base=0, channel_multiplier=1))
        nc.tensor.wait_ge(c.sPL.h, ev_sel)
        nc.vector.memset(hcarry[:], 0.0)
        ev_hz = c.sDVE.inc(nc.vector.memset(halo[:], 0.0))
        nc.vector.wait_ge(c.sDVE.h, ev_hz)

        TCA = 512
        NSUBA = TCA // NMM
        if not passes[0]:
            return
        c.barrier()
        with contextlib.ExitStack() as esA:
            xin = [esA.enter_context(sbuf(nc, f"mb_xi{i}", [128, TCA + 3], F32)) for i in range(2)]
            u32 = [esA.enter_context(sbuf(nc, f"mb_u{i}", [128, TCA], F32)) for i in range(NDT)]
            ubf = [esA.enter_context(sbuf(nc, f"mb_ub{i}", [128, TCA], BF16)) for i in range(NDT)]
            resb = [esA.enter_context(sbuf(nc, f"mb_rs{i}", [128, TCA], BF16)) for i in range(2)]
            dtbf = esA.enter_context(sbuf(nc, "mb_dtb", [64, TCA], BF16))
            drot = [esA.enter_context(sbuf(nc, f"mb_dr{i}", [128, TCA], F32)) for i in range(2)]
            durot = [esA.enter_context(sbuf(nc, f"mb_du{i}", [128, TCA], F32)) for i in range(2)]
            tt1 = esA.enter_context(sbuf(nc, "mb_t1", [128, TCA], F32))
            tt2 = esA.enter_context(sbuf(nc, "mb_t2", [128, TCA], F32))
            xnt = [esA.enter_context(sbuf(nc, f"mb_xn{i}", [128, TCA], BF16)) for i in range(8)]
            wpan = [esA.enter_context(sbuf(nc, f"mb_wp{i}", [128, 8 * 128], BF16)) for i in range(2)]
            ps = [esA.enter_context(psum(nc, f"mb_ps{i}", [128, NMM], F32)) for i in range(4)]
            psd = [esA.enter_context(psum(nc, f"mb_psd{i}", [64, NMM], F32)) for i in range(2)]
            psb = [esA.enter_context(psum(nc, f"mb_psb{i}", [32, NMM], F32)) for i in range(2)]
            for ch in range(S // TCA):
                t0 = ch * TCA
                ev_xn = 0
                for d in range(8):
                    c.gwait(nc.sync, "mb_xnt")
                    ev_xn = c.sX.inc(nc.sync.dma_start(
                        out=xnt[d][:], in_=xn_hbm[d * 128:(d + 1) * 128, t0:t0 + TCA]), 16)
                for m in range(32):
                    pb = wpan[m % 2]
                    c.gwait(nc.sync, f"mb_wp{m % 2}")
                    ev_p = c.sW.inc(nc.sync.dma_start(out=pb[:], in_=w["winp"][m]), 16)
                    nc.tensor.wait_ge(c.sW.h, ev_p)
                    nc.tensor.wait_ge(c.sX.h, ev_xn)
                    for n2 in range(NSUBA):
                        slot = (m * NSUBA + n2) % 4
                        c.gwait(nc.tensor, f"mb_ps{slot}")
                        for k in range(8):
                            mm = nc.tensor.matmul(ps[slot][:], pb[:, k * 128:(k + 1) * 128],
                                                  xnt[k][:, n2 * NMM:(n2 + 1) * NMM],
                                                  start=(k == 0), stop=(k == 7))
                        ev_mm = pe_mark(c, mm)
                        nc.scalar.wait_ge(c.sPE.h, ev_mm)
                        if m < 16:
                            xb = xin[m % 2]
                            if n2 == 0:
                                c.gwait(nc.scalar, f"mb_xi{m % 2}")
                                nc.scalar.wait_ge(c.sDVE.h, c.guards.get(f"mb_halo{m}", (c.sDVE.h, 0))[1])
                                ev_hc = c.sACT.inc(nc.scalar.copy(xb[:, 0:3], halo[:, m * 3:m * 3 + 3]))
                            ev_e = c.sACT.inc(nc.scalar.copy(
                                xb[:, 3 + n2 * NMM:3 + (n2 + 1) * NMM], ps[slot][:]))
                            c.gset(f"mb_ps{slot}", c.sACT, ev_e)
                        else:
                            rb = resb[m % 2]
                            c.gwait(nc.scalar, f"mb_rs{m % 2}") if n2 == 0 else None
                            ev_e = c.sACT.inc(nc.scalar.activation(
                                rb[:, n2 * NMM:(n2 + 1) * NMM], ps[slot][:], AF.Silu))
                            c.gset(f"mb_ps{slot}", c.sACT, ev_e)
                    if m < 16:
                        dt = m
                        xb = xin[m % 2]
                        nc.vector.wait_ge(c.sACT.h, ev_e)
                        ev_hs = c.sDVE.inc(nc.vector.tensor_copy(
                            halo[:, dt * 3:dt * 3 + 3], xb[:, TCA:TCA + 3]))
                        c.gset(f"mb_halo{dt}", c.sDVE, ev_hs)
                        ua = u32[dt]
                        c.gwait(nc.vector, f"mb_u{dt}")
                        nc.vector.tensor_scalar(ua[:], xb[:, 0:TCA], cw[:, dt * 4:dt * 4 + 1],
                                                cb[:, dt:dt + 1], ALU.mult, ALU.add)
                        for kk in range(1, 4):
                            nc.vector.scalar_tensor_tensor(
                                ua[:], xb[:, kk:kk + TCA], cw[:, dt * 4 + kk:dt * 4 + kk + 1],
                                ua[:], ALU.mult, ALU.add)
                        ev_cv = c.sDVE.inc(nc.vector.tensor_copy(tt1[0:1, 0:1], ua[0:1, 0:1]))
                        c.gset(f"mb_xi{m % 2}", c.sDVE, ev_cv)
                        nc.scalar.wait_ge(c.sDVE.h, ev_cv)
                        ev_su = c.sACT.inc(nc.scalar.activation(ua[:], ua[:], AF.Silu))
                        nc.vector.wait_ge(c.sACT.h, ev_su)
                        c.gwait(nc.vector, f"mb_ubf{dt}")
                        ev_ub = c.sDVE.inc(nc.vector.tensor_copy(ubf[dt][:], ua[:]))
                    else:
                        rt = m - 16
                        nc.sync.wait_ge(c.sACT.h, ev_e)
                        ev_st = c.sX.inc(nc.sync.dma_start(
                            out=scr["res"][rt * 128:(rt + 1) * 128, t0:t0 + TCA],
                            in_=resb[m % 2][:]), 16)
                        c.gset(f"mb_rs{m % 2}", c.sX, ev_st)
                    c.gset(f"mb_wp{m % 2}", c.sPE, c.sPE.n)
                c.gset("mb_xnt", c.sPE, c.sPE.n)
                ev_ub_all = c.sDVE.n

                nc.tensor.wait_ge(c.sDVE.h, ev_ub_all)
                for n2 in range(NSUBA):
                    c.gwait(nc.tensor, f"mb_psd{n2}")
                    c.gwait(nc.tensor, f"mb_psb{n2}")
                    for k in range(NDT):
                        nc.tensor.matmul(psd[n2][:], wx_sb[:, k * 96:k * 96 + 64],
                                         ubf[k][:, n2 * NMM:(n2 + 1) * NMM],
                                         start=(k == 0), stop=(k == NDT - 1))
                    for k in range(NDT):
                        mm = nc.tensor.matmul(psb[n2][:], wx_sb[:, k * 96 + 64:k * 96 + 96],
                                              ubf[k][:, n2 * NMM:(n2 + 1) * NMM],
                                              start=(k == 0), stop=(k == NDT - 1))
                    ev_mm = pe_mark(c, mm)
                    nc.scalar.wait_ge(c.sPE.h, ev_mm)
                    c.gwait(nc.scalar, "mb_dtb") if n2 == 0 else None
                    ev_dt = c.sACT.inc(nc.scalar.copy(dtbf[:, n2 * NMM:(n2 + 1) * NMM], psd[n2][:]))
                    c.gset(f"mb_psd{n2}", c.sACT, ev_dt)
                    nc.vector.wait_ge(c.sPE.h, ev_mm)
                    ev_bc = c.sDVE.inc(nc.vector.tensor_copy(
                        bc_sb[:, t0 + n2 * NMM:t0 + (n2 + 1) * NMM], psb[n2][:]))
                    c.gset(f"mb_psb{n2}", c.sDVE, ev_bc)
                ev_dt_all = c.sACT.n
                for _dt in range(NDT):
                    c.gset(f"mb_ubf{_dt}", c.sPE, c.sPE.n)

                nc.tensor.wait_ge(c.sACT.h, ev_dt_all)
                for dt in range(NDT):
                    dr = drot[dt % 2]
                    du = durot[dt % 2]
                    for n2 in range(NSUBA):
                        slot = (dt * NSUBA + n2) % 4
                        c.gwait(nc.tensor, f"mb_ps{slot}")
                        mm = nc.tensor.matmul(ps[slot][:], wdt_sb[:, dt * 128:(dt + 1) * 128],
                                              dtbf[:, n2 * NMM:(n2 + 1) * NMM],
                                              start=True, stop=True)
                        ev_mm = pe_mark(c, mm)
                        sl = slice(n2 * NMM, (n2 + 1) * NMM)
                        nc.scalar.wait_ge(c.sPE.h, ev_mm)
                        if n2 == 0:
                            c.gwait(nc.scalar, f"mb_dr{dt % 2}")
                        c.gwait(nc.scalar, "mb_tt")
                        nc.scalar.activation(tt1[:, sl], ps[slot][:], AF.Abs, bias=bdt[:, dt:dt + 1])
                        nc.scalar.activation(tt1[:, sl], tt1[:, sl], AF.Exp, scale=-1.0)
                        nc.scalar.activation(tt1[:, sl], tt1[:, sl], AF.Ln, bias=1.0)
                        ev_r = c.sACT.inc(nc.scalar.activation(
                            tt2[:, sl], ps[slot][:], AF.Relu, bias=bdt[:, dt:dt + 1]))
                        c.gset(f"mb_ps{slot}", c.sACT, ev_r)
                        nc.vector.wait_ge(c.sACT.h, ev_r)
                        ev_add = c.sDVE.inc(nc.vector.tensor_add(dr[:, sl], tt1[:, sl], tt2[:, sl]))
                        c.gset("mb_tt", c.sDVE, ev_add)
                    nc.vector.scalar_tensor_tensor(du[:], u32[dt][:], 1.0, dr[:], ALU.mult, ALU.mult) if False else None
                    nc.vector.tensor_mul(du[:], dr[:], u32[dt][:])
                    ev_ud = c.sDVE.inc(nc.vector.tensor_scalar(
                        u32[dt][:], u32[dt][:], dcol[:, dt:dt + 1], None, ALU.mult))
                    nc.sync.wait_ge(c.sDVE.h, ev_ud)
                    c.sX.inc(nc.sync.dma_start(
                        out=scr["delta"][dt * 128:(dt + 1) * 128, t0:t0 + TCA], in_=dr[:]), 16)
                    c.sX.inc(nc.sync.dma_start(
                        out=scr["du"][dt * 128:(dt + 1) * 128, t0:t0 + TCA], in_=du[:]), 16)
                    ev_st = c.sX.inc(nc.sync.dma_start(
                        out=scr["uD"][dt * 128:(dt + 1) * 128, t0:t0 + TCA], in_=u32[dt][:]), 16)
                    c.gset(f"mb_dr{dt % 2}", c.sX, ev_st)
                    c.gset(f"mb_u{dt}", c.sX, ev_st)

        if not passes[1]:
            return
        c.barrier()
        TQ = 1024
        NQ = S // TQ
        with contextlib.ExitStack() as esB:
            dl = [esB.enter_context(sbuf(nc, f"sc_d{i}", [128, S], F32)) for i in range(2)]
            dul = [esB.enter_context(sbuf(nc, f"sc_u{i}", [128, S], F32)) for i in range(2)]
            yb = [esB.enter_context(sbuf(nc, f"sc_y{i}", [128, S], F32)) for i in range(2)]
            a_sb = esB.enter_context(sbuf(nc, "sc_a", [128, TQ], F32))
            b_sb = esB.enter_context(sbuf(nc, "sc_b", [128, TQ], F32))
            h_sb = esB.enter_context(sbuf(nc, "sc_h", [128, TQ], F32))
            z_sb = esB.enter_context(sbuf(nc, "sc_z", [128, TQ], F32))
            psB = [esB.enter_context(psum(nc, f"sc_pb{i}", [128, TQ], F32)) for i in range(2)]
            psC = [esB.enter_context(psum(nc, f"sc_pc{i}", [128, TQ], F32)) for i in range(2)]
            for g in range(NDT // 2):
                dts = (2 * g, 2 * g + 1)
                ev_ld = 0
                for i, dt in enumerate(dts):
                    c.gwait(nc.sync, f"sc_ld{i}")
                    c.sX.inc(nc.sync.dma_start(
                        out=dl[i][:], in_=scr["delta"][dt * 128:(dt + 1) * 128, :]), 16)
                    ev_ld = c.sX.inc(nc.sync.dma_start(
                        out=dul[i][:], in_=scr["du"][dt * 128:(dt + 1) * 128, :]), 16)
                for n in range(NST):
                    for q in range(NQ):
                        qsl = slice(q * TQ, (q + 1) * TQ)
                        pb_ = psB[q % 2]
                        pc_ = psC[q % 2]
                        c.gwait(nc.tensor, f"sc_pb{q % 2}")
                        c.gwait(nc.tensor, f"sc_pc{q % 2}")
                        for n2 in range(TQ // NMM):
                            ssl = slice(q * TQ + n2 * NMM, q * TQ + (n2 + 1) * NMM)
                            nc.tensor.matmul(pb_[:, n2 * NMM:(n2 + 1) * NMM],
                                             sel[:, n * 128:(n + 1) * 128],
                                             bc_sb[:, ssl], start=True, stop=True)
                            mm = nc.tensor.matmul(pc_[:, n2 * NMM:(n2 + 1) * NMM],
                                                  sel[:, (16 + n) * 128:(16 + n + 1) * 128],
                                                  bc_sb[:, ssl], start=True, stop=True)
                        ev_bc = pe_mark(c, mm)
                        for i, dt in enumerate(dts):
                            if g == 0 and n == 0 and q == 0 and i == 0:
                                nc.scalar.wait_ge(c.sX.h, ev_ld)
                            nc.scalar.wait_ge(c.sX.h, ev_ld)
                            c.gwait(nc.scalar, "sc_a")
                            ev_a = c.sACT.inc(nc.scalar.activation(
                                a_sb[:], dl[i][:, qsl], AF.Exp,
                                scale=negA[:, dt * 16 + n:dt * 16 + n + 1]))
                            nc.vector.wait_ge(c.sACT.h, ev_a)
                            nc.vector.wait_ge(c.sPE.h, ev_bc)
                            c.gwait(nc.vector, "sc_b")
                            ev_b = c.sDVE.inc(nc.vector.tensor_mul(b_sb[:], dul[i][:, qsl], pb_[:]))
                            nc.vector.wait_ge(c.sDVE.h, ev_b)
                            c.gwait(nc.vector, "sc_hcar")
                            ev_sc = c.sDVE.inc(nc.vector.tensor_tensor_scan(
                                h_sb[:], a_sb[:], b_sb[:],
                                hcarry[:, dt * 16 + n:dt * 16 + n + 1], ALU.mult, ALU.add))
                            nc.vector.wait_ge(c.sDVE.h, ev_sc)
                            ev_hc = c.sDVE.inc(nc.vector.tensor_copy(
                                hcarry[:, dt * 16 + n:dt * 16 + n + 1], h_sb[:, TQ - 1:TQ]))
                            c.gset("sc_hcar", c.sDVE, ev_hc)
                            c.gset("sc_a", c.sDVE, ev_hc)
                            c.gset("sc_b", c.sDVE, ev_hc)
                            nc.vector.tensor_mul(z_sb[:], h_sb[:], pc_[:])
                            if n == 0:
                                ev_y = c.sDVE.inc(nc.vector.tensor_copy(yb[i][:, qsl], z_sb[:]))
                            else:
                                ev_y = c.sDVE.inc(nc.vector.tensor_add(yb[i][:, qsl], yb[i][:, qsl], z_sb[:]))
                        c.gset(f"sc_pb{q % 2}", c.sDVE, ev_y)
                        c.gset(f"sc_pc{q % 2}", c.sDVE, ev_y)
                for i, dt in enumerate(dts):
                    nc.sync.wait_ge(c.sDVE.h, c.sDVE.n)
                    ev_sy = c.sX.inc(nc.sync.dma_start(
                        out=scr["y"][dt * 128:(dt + 1) * 128, :], in_=yb[i][:]), 16)
                    c.gset(f"sc_ld{i}", c.sX, ev_sy)

        if not passes[2]:
            return
        c.barrier()
        with contextlib.ExitStack() as esC:
            yr = [esC.enter_context(sbuf(nc, f"mc_yr{i}", [128, TC], BF16)) for i in range(NDT)]
            ty = esC.enter_context(sbuf(nc, "mc_ty", [128, TC], F32))
            tu = esC.enter_context(sbuf(nc, "mc_tu", [128, TC], F32))
            trs = esC.enter_context(sbuf(nc, "mc_rs", [128, TC], BF16))
            xr = esC.enter_context(sbuf(nc, "mc_xr", [128, TC], F32))
            xo = esC.enter_context(sbuf(nc, "mc_xo", [128, TC], F32))
            wpan = [esC.enter_context(sbuf(nc, f"mc_wp{i}", [128, 16 * 128], BF16)) for i in range(2)]
            ps = [esC.enter_context(psum(nc, f"mc_ps{i}", [128, NMM], F32)) for i in range(4)]
            for ch in range(S // TC):
                t0 = ch * TC
                for dt in range(NDT):
                    c.gwait(nc.sync, "mc_ty")
                    c.sX.inc(nc.sync.dma_start(
                        out=ty[:], in_=scr["y"][dt * 128:(dt + 1) * 128, t0:t0 + TC]), 16)
                    c.sX.inc(nc.sync.dma_start(
                        out=tu[:], in_=scr["uD"][dt * 128:(dt + 1) * 128, t0:t0 + TC]), 16)
                    ev_l = c.sX.inc(nc.sync.dma_start(
                        out=trs[:], in_=scr["res"][dt * 128:(dt + 1) * 128, t0:t0 + TC]), 16)
                    nc.vector.wait_ge(c.sX.h, ev_l)
                    c.gwait(nc.vector, f"mc_yr{dt}")
                    nc.vector.tensor_add(ty[:], ty[:], tu[:])
                    ev_yr = c.sDVE.inc(nc.vector.tensor_mul(yr[dt][:], ty[:], trs[:]))
                    c.gset("mc_ty", c.sDVE, ev_yr)
                ev_yr_all = c.sDVE.n
                nc.tensor.wait_ge(c.sDVE.h, ev_yr_all)
                for m in range(8):
                    pb = wpan[m % 2]
                    c.gwait(nc.sync, f"mc_wp{m % 2}")
                    ev_p = c.sW.inc(nc.sync.dma_start(out=pb[:], in_=w["woutp"][m]), 16)
                    c.gwait(nc.sync, "mc_xr")
                    ev_xr = c.sX.inc(nc.sync.dma_start(
                        out=xr[:], in_=x_hbm[m * 128:(m + 1) * 128, t0:t0 + TC]), 16)
                    nc.tensor.wait_ge(c.sW.h, ev_p)
                    for n2 in range(NSUB):
                        slot = (m * NSUB + n2) % 4
                        c.gwait(nc.tensor, f"mc_ps{slot}")
                        for k in range(NDT):
                            mm = nc.tensor.matmul(ps[slot][:], pb[:, k * 128:(k + 1) * 128],
                                                  yr[k][:, n2 * NMM:(n2 + 1) * NMM],
                                                  start=(k == 0), stop=(k == NDT - 1))
                        ev_mm = pe_mark(c, mm)
                        nc.vector.wait_ge(c.sPE.h, ev_mm)
                        nc.vector.wait_ge(c.sX.h, ev_xr)
                        c.gwait(nc.vector, "mc_xo")
                        ev_v = c.sDVE.inc(nc.vector.tensor_add(
                            xo[:, n2 * NMM:(n2 + 1) * NMM], ps[slot][:],
                            xr[:, n2 * NMM:(n2 + 1) * NMM]))
                        c.gset(f"mc_ps{slot}", c.sDVE, ev_v)
                    c.gset("mc_xr", c.sDVE, c.sDVE.n)
                    nc.sync.wait_ge(c.sDVE.h, c.sDVE.n)
                    ev_st = c.sX.inc(nc.sync.dma_start(
                        out=xout_hbm[m * 128:(m + 1) * 128, t0:t0 + TC], in_=xo[:]), 16)
                    c.gset("mc_xo", c.sX, ev_st)
                    c.gset(f"mc_wp{m % 2}", c.sPE, c.sPE.n)
                for dt in range(NDT):
                    c.gset(f"mc_yr{dt}", c.sPE, c.sPE.n)
        c.barrier()


def prep_mamba_host(inp):
    wx = inp["L0_attn_W_x"]
    return {
        "mb_winp": _panels(inp["L0_attn_W_in"], 1024, 4096),
        "mb_wx": np.ascontiguousarray(wx.reshape(16, 128, 96).transpose(1, 0, 2).reshape(128, 16 * 96)),
        "mb_wdt": np.ascontiguousarray(inp["L0_attn_W_dt"]),
        "mb_cw": np.ascontiguousarray(inp["L0_attn_conv_w"].reshape(16, 128, 4).transpose(1, 0, 2).reshape(128, 64)),
        "mb_cb": _cols(inp["L0_attn_conv_b"]),
        "mb_bdt": _cols(inp["L0_attn_b_dt"]),
        "mb_dcol": _cols(inp["L0_attn_D"]),
        "mb_acols": np.ascontiguousarray(inp["L0_attn_A_log"].reshape(16, 128, 16).transpose(1, 0, 2).reshape(128, 256)),
        "mb_woutp": _panels(inp["L0_attn_W_out"], 2048, 1024),
    }


def decl_mamba(nc, cl):
    winp = _wtens(nc, "mb_winp", [32, 128, 8 * 128], cl)
    woutp = _wtens(nc, "mb_woutp", [8, 128, 16 * 128], cl)
    return {
        "winp": [winp.ap()[m] for m in range(32)],
        "woutp": [woutp.ap()[m] for m in range(8)],
        "wx": _wtens(nc, "mb_wx", [128, 16 * 96], cl).ap(),
        "wdt": _wtens(nc, "mb_wdt", [64, DI], cl).ap(),
        "cw": nc.dram_tensor("mb_cw", [128, 64], F32, kind="ExternalInput").ap(),
        "cb": nc.dram_tensor("mb_cb", [128, 16], F32, kind="ExternalInput").ap(),
        "bdt": nc.dram_tensor("mb_bdt", [128, 16], F32, kind="ExternalInput").ap(),
        "dcol": nc.dram_tensor("mb_dcol", [128, 16], F32, kind="ExternalInput").ap(),
        "acols": nc.dram_tensor("mb_acols", [128, 256], F32, kind="ExternalInput").ap(),
    }


def decl_mamba_scratch(nc, kind="Internal"):
    return {
        "delta": nc.dram_tensor("mbs_delta", [DI, S], F32, kind=kind).ap(),
        "du": nc.dram_tensor("mbs_du", [DI, S], F32, kind=kind).ap(),
        "uD": nc.dram_tensor("mbs_uD", [DI, S], F32, kind=kind).ap(),
        "res": nc.dram_tensor("mbs_res", [DI, S], BF16, kind=kind).ap(),
        "y": nc.dram_tensor("mbs_y", [DI, S], F32, kind=kind).ap(),
    }


def build_phase_mamba(passes=(1, 1, 1), dbg=False):
    nc = bass.Bass("TRN2", target_bir_lowering=False)
    x_in = nc.dram_tensor("x_in", [D, S], F32, kind="ExternalInput")
    x_out = nc.dram_tensor("x_out", [D, S], F32, kind="ExternalOutput")
    xn_scr = nc.dram_tensor("xn_scr", [D, S], BF16)
    cl = []
    w = decl_mamba(nc, cl)
    scr = decl_mamba_scratch(nc, kind="ExternalOutput" if dbg else "Internal")
    with contextlib.ExitStack() as es:
        c = Ctx(nc, es)
        emit_casts(c, cl)
        emit_ln(c, x_in.ap(), xn_scr.ap())
        emit_mamba(c, w, x_in.ap(), xn_scr.ap(), x_out.ap(), scr, passes=passes)
        nc.sync.wait_ge(c.sX.h, c.sX.n)
        nc.sync.wait_ge(c.sW.h, c.sW.n)
    return nc


def emit_moe(c, w, x_hbm, xn_hbm, xout_hbm, dbg_tokw=None, dbg_route=None, dbg_exp=None):
    nc = c.nc
    with contextlib.ExitStack() as es:
        tokwT = es.enter_context(sbuf(nc, "mo_tokwT", [8, S], F32))
        sel8 = es.enter_context(sbuf(nc, "mo_sel8", [8, 8 * 128], F32))
        idf32 = es.enter_context(sbuf(nc, "mo_idf", [128, 128], F32))
        gatep = es.enter_context(sbuf(nc, "mo_gate", [128, 64], BF16))
        iota8 = es.enter_context(sbuf(nc, "mo_iota", [128, 8], F32))
        nc.gpsimd.memset(sel8[:], 1.0)
        nc.gpsimd.affine_select(sel8[:], sel8[:], [[-1, 8], [0, 128]], ALU.is_equal, 0.0,
                                base=0, channel_multiplier=1)
        nc.gpsimd.memset(idf32[:], 1.0)
        nc.gpsimd.affine_select(idf32[:], idf32[:], [[-1, 128]], ALU.is_equal, 0.0,
                                base=0, channel_multiplier=1)
        ev_c = c.sPL.inc(nc.gpsimd.iota(iota8[:], [[1, 8]], channel_multiplier=0,
                                        allow_small_or_imprecise_dtypes=True))
        ev_g = c.sW.inc(nc.sync.dma_start(out=gatep[:], in_=w["gatep"]), 16)
        nc.tensor.wait_ge(c.sPL.h, ev_c)
        nc.tensor.wait_ge(c.sW.h, ev_g)
        nc.vector.wait_ge(c.sPL.h, ev_c)

        with contextlib.ExitStack() as esR:
            xk = [esR.enter_context(sbuf(nc, f"mo_xk{i}", [128, S], BF16)) for i in range(8)]
            lg = esR.enter_context(sbuf(nc, "mo_lg", [128, 8], F32))
            mx8 = esR.enter_context(sbuf(nc, "mo_mx8", [128, 8], F32))
            ixu = esR.enter_context(sbuf(nc, "mo_ixu", [128, 8], U32))
            ixf = esR.enter_context(sbuf(nc, "mo_ixf", [128, 2], F32))
            e2 = esR.enter_context(sbuf(nc, "mo_e2", [128, 1], F32))
            w1v = esR.enter_context(sbuf(nc, "mo_w1", [128, 1], F32))
            w2v = esR.enter_context(sbuf(nc, "mo_w2", [128, 1], F32))
            oh = esR.enter_context(sbuf(nc, "mo_oh", [128, 8], F32))
            tkw = esR.enter_context(sbuf(nc, "mo_tkw", [128, 8], F32))
            lg_ps = esR.enter_context(psum(nc, "mo_lgps", [128, 8], F32))
            dbg_route_sb = esR.enter_context(sbuf(nc, "mo_dbgr", [128, 28], F32)) if dbg_route is not None else None
            tw_ps = esR.enter_context(psum(nc, "mo_twps", [8, 128], F32))
            ev_x = 0
            for k in range(8):
                ev_x = c.sX.inc(nc.sync.dma_start(
                    out=xk[k][:], in_=xn_hbm[k * 128:(k + 1) * 128, :]), 16)
            nc.tensor.wait_ge(c.sX.h, ev_x)
            ones8 = esR.enter_context(sbuf(nc, "mo_ones8", [128, 8], F32))
            onec = esR.enter_context(sbuf(nc, "mo_onec", [128, 1], F32))
            nc.gpsimd.memset(ones8[:], 1.0)
            ev_o8 = c.sPL.inc(nc.gpsimd.memset(onec[:], 1.0))
            nc.vector.wait_ge(c.sPL.h, ev_o8)
            ixb = esR.enter_context(sbuf(nc, "mo_ixb", [128, 8], F32))
            wbx = esR.enter_context(sbuf(nc, "mo_wbx", [128, 8], F32))

            def V(op, *args):
                ev = c.sDVE.inc(getattr(nc.vector, op)(*args))
                nc.vector.wait_ge(c.sDVE.h, ev)
                return ev

            for tt in range(32):
                c.gwait(nc.tensor, "mo_lgps")
                for k in range(8):
                    mm = nc.tensor.matmul(lg_ps[:], xk[k][:, tt * 128:(tt + 1) * 128],
                                          gatep[:, k * 8:(k + 1) * 8],
                                          start=(k == 0), stop=(k == 7))
                ev_mm = pe_mark(c, mm)
                nc.vector.wait_ge(c.sPE.h, ev_mm)
                c.gwait(nc.vector, "mo_route")
                ev_lg = V("tensor_copy", lg[:], lg_ps[:])
                c.gset("mo_lgps", c.sDVE, ev_lg)
                V("max", mx8[:], lg[:])
                V("max_index", ixu[:], mx8[:], lg[:])
                V("tensor_copy", ixf[:], ixu[:, 0:2])
                ev_d = V("tensor_sub", e2[:], mx8[:, 1:2], mx8[:, 0:1])
                nc.scalar.wait_ge(c.sDVE.h, ev_d)
                ev_e2 = c.sACT.inc(nc.scalar.activation(e2[:], e2[:], AF.Exp))
                nc.vector.wait_ge(c.sACT.h, ev_e2)
                V("tensor_add", w2v[:], e2[:], onec[:])
                V("reciprocal", w1v[:], w2v[:])
                V("tensor_mul", w2v[:], e2[:], w1v[:])
                for j in range(8):
                    V("tensor_copy", ixb[:, j:j + 1], ixf[:, 0:1])
                    V("tensor_copy", wbx[:, j:j + 1], w1v[:, 0:1])
                V("tensor_sub", oh[:], iota8[:], ixb[:])
                V("tensor_mul", oh[:], oh[:], oh[:])
                V("tensor_tensor", oh[:], oh[:], ones8[:], ALU.min)
                V("tensor_sub", oh[:], ones8[:], oh[:])
                V("tensor_mul", tkw[:], oh[:], wbx[:])
                for j in range(8):
                    V("tensor_copy", ixb[:, j:j + 1], ixf[:, 1:2])
                    V("tensor_copy", wbx[:, j:j + 1], w2v[:, 0:1])
                V("tensor_sub", oh[:], iota8[:], ixb[:])
                V("tensor_mul", oh[:], oh[:], oh[:])
                V("tensor_tensor", oh[:], oh[:], ones8[:], ALU.min)
                V("tensor_sub", oh[:], ones8[:], oh[:])
                V("tensor_mul", oh[:], oh[:], wbx[:])
                ev_tkw = V("tensor_add", tkw[:], tkw[:], oh[:])
                c.gset("mo_route", c.sDVE, ev_tkw)
                nc.tensor.wait_ge(c.sDVE.h, ev_tkw)
                c.gwait(nc.tensor, "mo_twps")
                mm = nc.tensor.matmul(tw_ps[:], tkw[:], idf32[:], start=True, stop=True)
                ev_tp = pe_mark(c, mm)
                nc.vector.wait_ge(c.sPE.h, ev_tp)
                ev_tw = c.sDVE.inc(nc.vector.tensor_copy(
                    tokwT[:, tt * 128:(tt + 1) * 128], tw_ps[:]))
                c.gset("mo_twps", c.sDVE, ev_tw)
            c.barrier()

        with contextlib.ExitStack() as esE:
            xnt = [esE.enter_context(sbuf(nc, f"mo_xn{i}", [128, TC], BF16)) for i in range(8)]
            xw = [esE.enter_context(sbuf(nc, f"mo_xw{i}", [128, TC], BF16)) for i in range(8)]
            hid = [esE.enter_context(sbuf(nc, f"mo_h{i}", [128, TC], BF16)) for i in range(32)]
            sg = esE.enter_context(sbuf(nc, "mo_sg", [128, TC], BF16))
            eoacc = [esE.enter_context(sbuf(nc, f"mo_eo{i}", [128, TC], F32)) for i in range(8)]
            xr = esE.enter_context(sbuf(nc, "mo_xr", [128, TC], F32))
            xo = esE.enter_context(sbuf(nc, "mo_xo", [128, TC], F32))
            p13 = [esE.enter_context(sbuf(nc, f"mo_p13{i}", [128, 8 * 128], BF16)) for i in range(4)]
            p2 = [esE.enter_context(sbuf(nc, f"mo_p2{i}", [128, 32 * 128], BF16)) for i in range(2)]
            ps = [esE.enter_context(psum(nc, f"mo_ps{i}", [128, NMM], F32)) for i in range(6)]
            wbc = esE.enter_context(psum(nc, "mo_wbc", [128, TC], F32))
            for ch in range(S // TC):
                t0 = ch * TC
                ev_xn = 0
                for d in range(8):
                    c.gwait(nc.sync, "mo_xnt")
                    ev_xn = c.sX.inc(nc.sync.dma_start(
                        out=xnt[d][:], in_=xn_hbm[d * 128:(d + 1) * 128, t0:t0 + TC]), 16)
                for e in range(NEXP):
                    c.gwait(nc.tensor, "mo_wbc")
                    for n2 in range(NSUB):
                        mm = nc.tensor.matmul(
                            wbc[:, n2 * NMM:(n2 + 1) * NMM],
                            sel8[:, e * 128:(e + 1) * 128],
                            tokwT[:, t0 + n2 * NMM:t0 + (n2 + 1) * NMM],
                            start=True, stop=True)
                    ev_bc = pe_mark(c, mm)
                    nc.vector.wait_ge(c.sPE.h, ev_bc)
                    nc.vector.wait_ge(c.sX.h, ev_xn)
                    for k in range(8):
                        c.gwait(nc.vector, f"mo_xw{k}")
                        ev_xw = c.sDVE.inc(nc.vector.tensor_mul(xw[k][:], xnt[k][:], wbc[:]))
                    c.gset("mo_wbc", c.sDVE, ev_xw)
                    _n_xw = c.sPE.n
                    for m in range(32):
                        pb1 = p13[m % 2]
                        pb3 = p13[2 + m % 2]
                        c.gwait(nc.sync, f"mo_p13{m % 2}")
                        c.sW.inc(nc.sync.dma_start(out=pb1[:], in_=w["w1p"][e][m]), 16)
                        c.gwait(nc.sync, f"mo_p13{2 + m % 2}")
                        ev_p = c.sW.inc(nc.sync.dma_start(out=pb3[:], in_=w["w3p"][e][m]), 16)
                        nc.tensor.wait_ge(c.sW.h, ev_p)
                        nc.tensor.wait_ge(c.sDVE.h, ev_xw)
                        for n2 in range(NSUB):
                            sg_ = (2 * (m * NSUB + n2)) % 4
                            su_ = (2 * (m * NSUB + n2) + 1) % 4
                            c.gwait(nc.tensor, f"mo_ps{sg_}")
                            c.gwait(nc.tensor, f"mo_ps{su_}")
                            for k in range(8):
                                nc.tensor.matmul(ps[sg_][:], pb1[:, k * 128:(k + 1) * 128],
                                                 xnt[k][:, n2 * NMM:(n2 + 1) * NMM],
                                                 start=(k == 0), stop=(k == 7))
                            for k in range(8):
                                mm = nc.tensor.matmul(ps[su_][:], pb3[:, k * 128:(k + 1) * 128],
                                                      xw[k][:, n2 * NMM:(n2 + 1) * NMM],
                                                      start=(k == 0), stop=(k == 7))
                            ev_mm = pe_mark(c, mm)
                            nc.scalar.wait_ge(c.sPE.h, ev_mm)
                            c.gwait(nc.scalar, "mo_sg")
                            ev_a = c.sACT.inc(nc.scalar.activation(
                                sg[:, n2 * NMM:(n2 + 1) * NMM], ps[sg_][:], AF.Silu))
                            nc.vector.wait_ge(c.sACT.h, ev_a)
                            c.gwait(nc.vector, f"mo_hid{m}") if n2 == 0 else None
                            ev_v = c.sDVE.inc(nc.vector.tensor_mul(
                                hid[m][:, n2 * NMM:(n2 + 1) * NMM], ps[su_][:],
                                sg[:, n2 * NMM:(n2 + 1) * NMM]))
                            c.gset(f"mo_ps{sg_}", c.sACT, ev_a)
                            c.gset(f"mo_ps{su_}", c.sDVE, ev_v)
                            c.gset("mo_sg", c.sDVE, ev_v)
                        c.gset(f"mo_p13{m % 2}", c.sPE, c.sPE.n)
                        c.gset(f"mo_p13{2 + m % 2}", c.sPE, c.sPE.n)
                    for k in range(8):
                        c.gset(f"mo_xw{k}", c.sPE, c.sPE.n)
                    ev_hid_done = (c.sDVE.h, c.sDVE.n)
                    for m in range(8):
                        pb = p2[m % 2]
                        c.gwait(nc.sync, f"mo_p2{m % 2}")
                        ev_p = c.sW.inc(nc.sync.dma_start(out=pb[:], in_=w["w2p"][e][m]), 16)
                        nc.tensor.wait_ge(c.sW.h, ev_p)
                        nc.tensor.wait_ge(ev_hid_done[0], ev_hid_done[1])
                        for n2 in range(NSUB):
                            slot = 4 + (m * NSUB + n2) % 2
                            c.gwait(nc.tensor, f"mo_ps{slot}")
                            for k in range(32):
                                mm = nc.tensor.matmul(ps[slot][:], pb[:, k * 128:(k + 1) * 128],
                                                      hid[k][:, n2 * NMM:(n2 + 1) * NMM],
                                                      start=(k == 0), stop=(k == 31))
                            ev_mm = pe_mark(c, mm)
                            nc.vector.wait_ge(c.sPE.h, ev_mm)
                            sl = slice(n2 * NMM, (n2 + 1) * NMM)
                            if e == 0:
                                ev_v = c.sDVE.inc(nc.vector.tensor_copy(eoacc[m][:, sl], ps[slot][:]))
                            else:
                                ev_v = c.sDVE.inc(nc.vector.tensor_add(
                                    eoacc[m][:, sl], eoacc[m][:, sl], ps[slot][:]))
                            c.gset(f"mo_ps{slot}", c.sDVE, ev_v)
                        c.gset(f"mo_p2{m % 2}", c.sPE, c.sPE.n)
                    for m in range(32):
                        c.gset(f"mo_hid{m}", c.sPE, c.sPE.n)
                if dbg_exp is not None and ch == 0:
                    nc.gpsimd.wait_ge(c.sDVE.h, c.sDVE.n)
                    c.sW.inc(nc.sync.dma_start(out=dbg_exp[0:128, 0:TC], in_=xw[0][:]), 16)
                    c.sW.inc(nc.sync.dma_start(out=dbg_exp[0:128, TC:2 * TC], in_=hid[0][:]), 16)
                    c.sW.inc(nc.sync.dma_start(out=dbg_exp[0:128, 2 * TC:3 * TC], in_=eoacc[0][:]), 16)
                for m in range(8):
                    c.gwait(nc.sync, "mo_xr")
                    ev_xr = c.sX.inc(nc.sync.dma_start(
                        out=xr[:], in_=x_hbm[m * 128:(m + 1) * 128, t0:t0 + TC]), 16)
                    nc.vector.wait_ge(c.sX.h, ev_xr)
                    c.gwait(nc.vector, "mo_xo")
                    ev_v = c.sDVE.inc(nc.vector.scalar_tensor_tensor(
                        xo[:], eoacc[m][:], 0.5, xr[:], ALU.mult, ALU.add))
                    c.gset("mo_xr", c.sDVE, ev_v)
                    nc.sync.wait_ge(c.sDVE.h, ev_v)
                    ev_st = c.sX.inc(nc.sync.dma_start(
                        out=xout_hbm[m * 128:(m + 1) * 128, t0:t0 + TC], in_=xo[:]), 16)
                    c.gset("mo_xo", c.sX, ev_st)
                c.gset("mo_xnt", c.sPE, c.sPE.n)
                c.barrier()
        c.barrier()


def prep_moe_host(inp, prefix="L1_ffn"):
    w1 = inp[f"{prefix}_experts_w1"]
    w2 = inp[f"{prefix}_experts_w2"]
    w3 = inp[f"{prefix}_experts_w3"]
    return {
        "moe_w1p": np.stack([_panels(w1[e], 1024, 4096) for e in range(NEXP)]),
        "moe_w3p": np.stack([_panels(w3[e], 1024, 4096) for e in range(NEXP)]),
        "moe_w2p": np.stack([_panels(w2[e], 4096, 1024) for e in range(NEXP)]),
        "moe_gatep": np.ascontiguousarray(
            inp[f"{prefix}_gate"].reshape(8, 128, 8).transpose(1, 0, 2).reshape(128, 64)),
    }


def decl_moe(nc, cl):
    w1p = _wtens(nc, "moe_w1p", [NEXP, 32, 128, 8 * 128], cl)
    w3p = _wtens(nc, "moe_w3p", [NEXP, 32, 128, 8 * 128], cl)
    w2p = _wtens(nc, "moe_w2p", [NEXP, 8, 128, 32 * 128], cl)
    return {
        "w1p": [[w1p.ap()[e, m] for m in range(32)] for e in range(NEXP)],
        "w3p": [[w3p.ap()[e, m] for m in range(32)] for e in range(NEXP)],
        "w2p": [[w2p.ap()[e, m] for m in range(8)] for e in range(NEXP)],
        "gatep": _wtens(nc, "moe_gatep", [128, 64], cl).ap(),
    }


def build_phase_moe(dbg=False):
    nc = bass.Bass("TRN2", target_bir_lowering=False)
    x_in = nc.dram_tensor("x_in", [D, S], F32, kind="ExternalInput")
    x_out = nc.dram_tensor("x_out", [D, S], F32, kind="ExternalOutput")
    xn_scr = nc.dram_tensor("xn_scr", [D, S], BF16)
    dbg_tokw = nc.dram_tensor("dbg_tokw", [8, S], F32, kind="ExternalOutput").ap() if dbg else None
    dbg_route = nc.dram_tensor("dbg_route", [128, 28], F32, kind="ExternalOutput").ap() if dbg else None
    dbg_exp = nc.dram_tensor("dbg_exp", [128, 3 * TC], F32, kind="ExternalOutput").ap() if dbg else None
    cl = []
    w = decl_moe(nc, cl)
    with contextlib.ExitStack() as es:
        c = Ctx(nc, es)
        emit_casts(c, cl)
        emit_ln(c, x_in.ap(), xn_scr.ap())
        emit_moe(c, w, x_in.ap(), xn_scr.ap(), x_out.ap(), dbg_tokw=dbg_tokw, dbg_route=dbg_route, dbg_exp=dbg_exp)
        nc.sync.wait_ge(c.sX.h, c.sX.n)
        nc.sync.wait_ge(c.sW.h, c.sW.n)
    return nc


def emit_attn(c, w, x_hbm, xn_hbm, xout_hbm, scr):
    nc = c.nc
    IC = 512
    with contextlib.ExitStack() as es:
        bq = es.enter_context(sbuf(nc, "at_bq", [128, 8], F32))
        bl = es.enter_context(sbuf(nc, "at_bl", [128, 2], F32))
        bk = es.enter_context(sbuf(nc, "at_bk", [128, 8], F32))
        bv = es.enter_context(sbuf(nc, "at_bv", [128, 8], F32))
        bo = es.enter_context(sbuf(nc, "at_bo", [128, 8], F32))
        ones2 = es.enter_context(sbuf(nc, "at_ones2", [128, 2], BF16))
        nmx = es.enter_context(sbuf(nc, "at_nmx", [2, 16], F32))
        qmx = es.enter_context(sbuf(nc, "at_qmx", [2, 16], F32))
        negMb = es.enter_context(sbuf(nc, "at_negM", [128, 16], F32))
        kT = [es.enter_context(sbuf(nc, f"at_k{i}", [128, S], BF16)) for i in range(8)]
        vsb = [es.enter_context(sbuf(nc, f"at_v{i}", [128, 16 * 65], BF16)) for i in range(32)]
        ev_b = 0
        for dst, srcn in ((bq, "bq"), (bl, "bl"), (bk, "bk"), (bv, "bv"), (bo, "bo")):
            ev_b = c.sX.inc(nc.sync.dma_start(out=dst[:], in_=w[srcn]), 16)
        nc.scalar.wait_ge(c.sX.h, ev_b)
        nc.gpsimd.memset(ones2[:], 0.0)
        nc.gpsimd.memset(ones2[0:64, 0:1], 1.0)
        ev_o2 = c.sPL.inc(nc.gpsimd.memset(ones2[64:128, 1:2], 1.0))
        nc.tensor.wait_ge(c.sPL.h, ev_o2)
        nc.vector.memset(qmx[:], 0.0)
        ev_qm = c.sDVE.inc(nc.vector.memset(nmx[:], 0.0))
        nc.vector.wait_ge(c.sDVE.h, ev_qm)
        for t in range(32):
            nc.gpsimd.memset(vsb[t][:, 64::65], 1.0)
        ev_vs1 = c.sPL.inc(nc.gpsimd.memset(vsb[31][:, 64:65], 1.0))

        with contextlib.ExitStack() as esP:
            xnt = [esP.enter_context(sbuf(nc, f"at_xn{i}", [128, TC], BF16)) for i in range(8)]
            latt = [esP.enter_context(sbuf(nc, f"at_lt{i}", [128, TC], BF16)) for i in range(2)]
            qbf = esP.enter_context(sbuf(nc, "at_qbf", [128, TC], BF16))
            vbf = esP.enter_context(sbuf(nc, "at_vbf", [128, TC], BF16))
            qsq = esP.enter_context(sbuf(nc, "at_qsq", [128, TC], BF16))
            nrm = esP.enter_context(sbuf(nc, "at_nrm", [2, 1], F32))
            wpan = [esP.enter_context(sbuf(nc, f"at_wp{i}", [128, 8 * 128], BF16)) for i in range(2)]
            ps = [esP.enter_context(psum(nc, f"at_ps{i}", [128, NMM], F32)) for i in range(4)]
            psn = esP.enter_context(psum(nc, "at_psn", [2, NMM], F32))
            pst = esP.enter_context(psum(nc, "at_pst", [128, 128], BF16))
            for ch in range(S // TC):
                t0 = ch * TC
                ev_xn = 0
                for d in range(8):
                    c.gwait(nc.sync, "at_xnt")
                    ev_xn = c.sX.inc(nc.sync.dma_start(
                        out=xnt[d][:], in_=xn_hbm[d * 128:(d + 1) * 128, t0:t0 + TC]), 16)

                def proj(panels, n_m, n_k, rhs_bufs, ev_rhs, rhs_sem, bias_t, evac):
                    for m in range(n_m):
                        pb = wpan[m % 2]
                        c.gwait(nc.sync, f"at_wp{m % 2}")
                        ev_p = c.sW.inc(nc.sync.dma_start(out=pb[:, 0:n_k * 128], in_=panels[m]), 16)
                        nc.tensor.wait_ge(c.sW.h, ev_p)
                        nc.tensor.wait_ge(rhs_sem, ev_rhs)
                        for n2 in range(NSUB):
                            slot = (m * NSUB + n2) % 4
                            c.gwait(nc.tensor, f"at_ps{slot}")
                            for k in range(n_k):
                                mm = nc.tensor.matmul(ps[slot][:], pb[:, k * 128:(k + 1) * 128],
                                                      rhs_bufs[k][:, n2 * NMM:(n2 + 1) * NMM],
                                                      start=(k == 0), stop=(k == n_k - 1))
                            ev_mm = pe_mark(c, mm)
                            evac(m, n2, slot, ev_mm, bias_t)
                        c.gset(f"at_wp{m % 2}", c.sPE, c.sPE.n)

                def evac_q(m, n2, slot, ev_mm, bias_t):
                    sl = slice(n2 * NMM, (n2 + 1) * NMM)
                    nc.scalar.wait_ge(c.sPE.h, ev_mm)
                    c.gwait(nc.scalar, "at_qbf")
                    ev_a = c.sACT.inc(nc.scalar.activation(
                        qbf[:, sl], ps[slot][:], AF.Identity, bias=bias_t[:, m:m + 1]))
                    ev_sq = c.sACT.inc(nc.scalar.activation(qsq[:, sl], qbf[:, sl], AF.Square))
                    c.gset(f"at_ps{slot}", c.sACT, ev_a)
                    nc.sync.wait_ge(c.sACT.h, ev_a)
                    ev_st = c.sX.inc(nc.sync.dma_start(
                        out=scr["qT"][m * 128:(m + 1) * 128, t0 + n2 * NMM:t0 + (n2 + 1) * NMM],
                        in_=qbf[:, sl]), 16)
                    c.gset("at_qbf", c.sX, ev_st)
                    nc.tensor.wait_ge(c.sACT.h, ev_sq)
                    c.gwait(nc.tensor, "at_psn")
                    mm2 = nc.tensor.matmul(psn[:], ones2[:], qsq[:, sl], start=True, stop=True)
                    ev_n = pe_mark(c, mm2)
                    nc.vector.wait_ge(c.sPE.h, ev_n)
                    c.gwait(nc.vector, "at_nrm")
                    e1 = c.sDVE.inc(nc.vector.reduce_max(nrm[:], psn[:], axis=AX.X))
                    nc.vector.wait_ge(c.sDVE.h, e1)
                    c.gset("at_psn", c.sDVE, e1)
                    e2_ = c.sDVE.inc(nc.vector.tensor_max(
                        qmx[:, m:m + 1], qmx[:, m:m + 1], nrm[:]))
                    nc.vector.wait_ge(c.sDVE.h, e2_)
                    c.gset("at_nrm", c.sDVE, e2_)

                proj(w["wqp"], 8, 8, xnt, ev_xn, c.sX.h, bq, evac_q)

                def evac_l(m, n2, slot, ev_mm, bias_t):
                    sl = slice(n2 * NMM, (n2 + 1) * NMM)
                    nc.scalar.wait_ge(c.sPE.h, ev_mm)
                    c.gwait(nc.scalar, f"at_lt{m}")
                    ev_a = c.sACT.inc(nc.scalar.activation(
                        latt[m][:, sl], ps[slot][:], AF.Identity, bias=bias_t[:, m:m + 1]))
                    c.gset(f"at_ps{slot}", c.sACT, ev_a)

                proj(w["wlp"], 2, 8, xnt, ev_xn, c.sX.h, bl, evac_l)
                ev_lat = (c.sACT.h, c.sACT.n)
                c.gset("at_xnt", c.sPE, c.sPE.n)

                def evac_k(m, n2, slot, ev_mm, bias_t):
                    sl = slice(t0 + n2 * NMM, t0 + (n2 + 1) * NMM)
                    sl2 = slice(n2 * NMM, (n2 + 1) * NMM)
                    nc.scalar.wait_ge(c.sPE.h, ev_mm)
                    ev_a = c.sACT.inc(nc.scalar.activation(
                        kT[m][:, sl], ps[slot][:], AF.Identity, bias=bias_t[:, m:m + 1]))
                    c.gwait(nc.scalar, "at_qbf")
                    ev_sq = c.sACT.inc(nc.scalar.activation(qsq[:, sl2], kT[m][:, sl], AF.Square))
                    c.gset(f"at_ps{slot}", c.sACT, ev_a)
                    nc.tensor.wait_ge(c.sACT.h, ev_sq)
                    c.gwait(nc.tensor, "at_psn")
                    mm2 = nc.tensor.matmul(psn[:], ones2[:], qsq[:, sl2], start=True, stop=True)
                    ev_n = pe_mark(c, mm2)
                    c.gset("at_qbf", c.sPE, ev_n)
                    nc.vector.wait_ge(c.sPE.h, ev_n)
                    c.gwait(nc.vector, "at_nrm")
                    e1 = c.sDVE.inc(nc.vector.reduce_max(nrm[:], psn[:], axis=AX.X))
                    nc.vector.wait_ge(c.sDVE.h, e1)
                    c.gset("at_psn", c.sDVE, e1)
                    e2_ = c.sDVE.inc(nc.vector.tensor_max(
                        qmx[:, 8 + m:9 + m], qmx[:, 8 + m:9 + m], nrm[:]))
                    nc.vector.wait_ge(c.sDVE.h, e2_)
                    c.gset("at_nrm", c.sDVE, e2_)

                nc.tensor.wait_ge(ev_lat[0], ev_lat[1])
                proj(w["wkp"], 8, 2, latt, ev_lat[1], ev_lat[0], bk, evac_k)

                def evac_v(m, n2, slot, ev_mm, bias_t):
                    sl = slice(n2 * NMM, (n2 + 1) * NMM)
                    nc.scalar.wait_ge(c.sPE.h, ev_mm)
                    if n2 == 0:
                        c.gwait(nc.scalar, "at_vbf")
                    ev_a = c.sACT.inc(nc.scalar.activation(
                        vbf[:, sl], ps[slot][:], AF.Identity, bias=bias_t[:, m:m + 1]))
                    c.gset(f"at_ps{slot}", c.sACT, ev_a)
                    if n2 == NSUB - 1:
                        nc.tensor.wait_ge(c.sACT.h, ev_a)
                        for par in range(2):
                            h = 2 * m + par
                            for jt in range(TC // 128):
                                tile = (t0 // 128) + jt
                                c.gwait(nc.tensor, "at_pst")
                                mmt = nc.tensor.transpose(
                                    pst[0:128, 0:64],
                                    vbf[64 * par:64 * par + 64, jt * 128:(jt + 1) * 128],
                                    c.ident_bf[64 * par:64 * par + 64, 64 * par:64 * par + 64])
                                ev_t = pe_mark(c, mmt)
                                nc.vector.wait_ge(c.sPE.h, ev_t)
                                ev_cv = c.sDVE.inc(nc.vector.tensor_copy(
                                    vsb[tile][:, 65 * h:65 * h + 64], pst[0:128, 0:64]))
                                c.gset("at_pst", c.sDVE, ev_cv)
                        c.gset("at_vbf", c.sPE, c.sPE.n)

                proj(w["wvp"], 8, 2, latt, ev_lat[1], ev_lat[0], bv, evac_v)
                for m in range(2):
                    c.gset(f"at_lt{m}", c.sPE, c.sPE.n)
            c.barrier()

        with contextlib.ExitStack() as esM:
            t28 = esM.enter_context(sbuf(nc, "at_t28", [2, 8], F32))
            r116 = esM.enter_context(sbuf(nc, "at_r116", [1, 16], F32))
            psm = esM.enter_context(psum(nc, "at_psm", [128, 16], F32))
            ev = c.sDVE.inc(nc.vector.tensor_mul(t28[:], qmx[:, 0:8], qmx[:, 8:16]))
            nc.scalar.wait_ge(c.sDVE.h, ev)
            ev = c.sACT.inc(nc.scalar.activation(t28[:], t28[:], AF.Sqrt))
            nc.vector.wait_ge(c.sACT.h, ev)
            ev = c.sDVE.inc(nc.vector.tensor_scalar(t28[:], t28[:], -SCALE_ATTN * 1.0001, None, ALU.mult))
            nc.sync.wait_ge(c.sDVE.h, ev)
            ev = c.sX.inc(nc.sync.dma_start(
                out=scr["m16"].rearrange("m p -> p m"), in_=t28[:],
                allow_slow_non_contiguous=True), 16)
            nc.sync.wait_ge(c.sX.h, ev)
            ev = c.sX.inc(nc.sync.dma_start(
                out=r116[:], in_=scr["m16"].rearrange("m p -> (m p)")[None, :]), 16)
            nc.tensor.wait_ge(c.sX.h, ev)
            mm = nc.tensor.matmul(psm[:], c.ones_f32[0:1, :], r116[:], start=True, stop=True)
            ev = pe_mark(c, mm)
            nc.vector.wait_ge(c.sPE.h, ev)
            ev = c.sDVE.inc(nc.vector.tensor_copy(negMb[:], psm[:]))
            nc.scalar.wait_ge(c.sDVE.h, ev)
            c.barrier()

        with contextlib.ExitStack() as esC:
            qh = esC.enter_context(sbuf(nc, "at_qh", [128, S], BF16))
            ex = [esC.enter_context(sbuf(nc, f"at_ex{i}", [128, IC], BF16)) for i in range(2)]
            recs = esC.enter_context(sbuf(nc, "at_recs", [128, IC], F32))
            recb = esC.enter_context(sbuf(nc, "at_recb", [64, IC], F32))
            ao = esC.enter_context(sbuf(nc, "at_ao", [64, IC], BF16))
            pss = [esC.enter_context(psum(nc, f"at_pss{i}", [128, IC], F32)) for i in range(2)]
            psa = [esC.enter_context(psum(nc, f"at_psa{i}", [65, IC], F32)) for i in range(2)]
            psr = esC.enter_context(psum(nc, "at_psr", [64, IC], F32))
            nc.tensor.wait_ge(c.sPL.h, ev_vs1)
            for h in range(16):
                kt_tile = kT[h // 2]
                kbase = 64 * (h % 2)
                c.gwait(nc.sync, "at_qh")
                ev_q = c.sX.inc(nc.sync.dma_start(
                    out=qh[kbase:kbase + 64, :], in_=scr["qT"][h * 64:(h + 1) * 64, :]), 16)
                for ic in range(S // IC):
                    pa = psa[ic % 2]
                    for jt in range(32):
                        pb_ = pss[jt % 2]
                        c.gwait(nc.tensor, f"at_pss{jt % 2}")
                        nc.tensor.wait_ge(c.sX.h, ev_q)
                        mm = nc.tensor.matmul(
                            pb_[:], kt_tile[kbase:kbase + 64, jt * 128:(jt + 1) * 128],
                            qh[kbase:kbase + 64, ic * IC:(ic + 1) * IC], start=True, stop=True)
                        ev_s = pe_mark(c, mm)
                        nc.scalar.wait_ge(c.sPE.h, ev_s)
                        c.gwait(nc.scalar, f"at_ex{jt % 2}")
                        ev_e = c.sACT.inc(nc.scalar.activation(
                            ex[jt % 2][:], pb_[:], AF.Exp, scale=SCALE_ATTN,
                            bias=negMb[:, h:h + 1]))
                        c.gset(f"at_pss{jt % 2}", c.sACT, ev_e)
                        nc.tensor.wait_ge(c.sACT.h, ev_e)
                        if jt == 0:
                            c.gwait(nc.tensor, f"at_psa{ic % 2}")
                        mm = nc.tensor.matmul(pa[:], vsb[jt][:, 65 * h:65 * h + 65],
                                              ex[jt % 2][:], start=(jt == 0), stop=(jt == 31))
                        ev_av = pe_mark(c, mm)
                        c.gset(f"at_ex{jt % 2}", c.sPE, ev_av)
                    nc.vector.wait_ge(c.sPE.h, ev_av)
                    c.gwait(nc.vector, "at_recs")
                    ev1 = c.sDVE.inc(nc.vector.tensor_copy(recs[64:65, :], pa[64:65, :]))
                    nc.vector.wait_ge(c.sDVE.h, ev1)
                    ev2 = c.sDVE.inc(nc.vector.reciprocal(recs[64:65, :], recs[64:65, :]))
                    nc.vector.wait_ge(c.sDVE.h, ev2)
                    nc.tensor.wait_ge(c.sDVE.h, ev2)
                    c.gwait(nc.tensor, "at_psr")
                    mm = nc.tensor.matmul(psr[:], c.ones_bf[64:65, 0:64] if False else c.ones_f32[64:65, 0:64],
                                          recs[64:65, :], start=True, stop=True)
                    ev3 = pe_mark(c, mm)
                    nc.vector.wait_ge(c.sPE.h, ev3)
                    c.gwait(nc.vector, "at_recb")
                    ev4 = c.sDVE.inc(nc.vector.tensor_copy(recb[:], psr[:]))
                    c.gset("at_psr", c.sDVE, ev4)
                    c.gset("at_recs", c.sDVE, ev4)
                    nc.vector.wait_ge(c.sDVE.h, ev4)
                    c.gwait(nc.vector, "at_ao")
                    ev5 = c.sDVE.inc(nc.vector.tensor_mul(ao[:], pa[0:64, :], recb[:]))
                    c.gset(f"at_psa{ic % 2}", c.sDVE, ev5)
                    c.gset("at_recb", c.sDVE, ev5)
                    nc.sync.wait_ge(c.sDVE.h, ev5)
                    ev_st = c.sX.inc(nc.sync.dma_start(
                        out=scr["aoT"][h * 64:(h + 1) * 64, ic * IC:(ic + 1) * IC],
                        in_=ao[:]), 16)
                    c.gset("at_ao", c.sX, ev_st)
                c.gset("at_qh", c.sPE, c.sPE.n)
            c.barrier()

        with contextlib.ExitStack() as esO:
            aot = [esO.enter_context(sbuf(nc, f"at_ao{i}", [128, TC], BF16)) for i in range(8)]
            xr = esO.enter_context(sbuf(nc, "at_xr", [128, TC], F32))
            xo = esO.enter_context(sbuf(nc, "at_xo", [128, TC], F32))
            tf = esO.enter_context(sbuf(nc, "at_tf", [128, TC], F32))
            wpan = [esO.enter_context(sbuf(nc, f"at_wo{i}", [128, 8 * 128], BF16)) for i in range(2)]
            ps = [esO.enter_context(psum(nc, f"at_po{i}", [128, NMM], F32)) for i in range(4)]
            for ch in range(S // TC):
                t0 = ch * TC
                ev_ao = 0
                for d in range(8):
                    c.gwait(nc.sync, "ao_ld")
                    ev_ao = c.sX.inc(nc.sync.dma_start(
                        out=aot[d][:], in_=scr["aoT"][d * 128:(d + 1) * 128, t0:t0 + TC]), 16)
                for m in range(8):
                    pb = wpan[m % 2]
                    c.gwait(nc.sync, f"at_wo{m % 2}")
                    ev_p = c.sW.inc(nc.sync.dma_start(out=pb[:], in_=w["wop"][m]), 16)
                    c.gwait(nc.sync, "at_xr")
                    ev_xr = c.sX.inc(nc.sync.dma_start(
                        out=xr[:], in_=x_hbm[m * 128:(m + 1) * 128, t0:t0 + TC]), 16)
                    nc.tensor.wait_ge(c.sW.h, ev_p)
                    nc.tensor.wait_ge(c.sX.h, ev_ao)
                    for n2 in range(NSUB):
                        slot = (m * NSUB + n2) % 4
                        c.gwait(nc.tensor, f"at_po{slot}")
                        for k in range(8):
                            mm = nc.tensor.matmul(ps[slot][:], pb[:, k * 128:(k + 1) * 128],
                                                  aot[k][:, n2 * NMM:(n2 + 1) * NMM],
                                                  start=(k == 0), stop=(k == 7))
                        ev_mm = pe_mark(c, mm)
                        sl = slice(n2 * NMM, (n2 + 1) * NMM)
                        nc.scalar.wait_ge(c.sPE.h, ev_mm)
                        c.gwait(nc.scalar, "at_tf")
                        ev_a = c.sACT.inc(nc.scalar.activation(
                            tf[:, sl], ps[slot][:], AF.Identity, bias=bo[:, m:m + 1]))
                        c.gset(f"at_po{slot}", c.sACT, ev_a)
                        nc.vector.wait_ge(c.sACT.h, ev_a)
                        nc.vector.wait_ge(c.sX.h, ev_xr)
                        c.gwait(nc.vector, "at_xo")
                        ev_v = c.sDVE.inc(nc.vector.tensor_add(xo[:, sl], tf[:, sl], xr[:, sl]))
                        c.gset("at_tf", c.sDVE, ev_v)
                    c.gset("at_xr", c.sDVE, c.sDVE.n)
                    nc.sync.wait_ge(c.sDVE.h, c.sDVE.n)
                    ev_st = c.sX.inc(nc.sync.dma_start(
                        out=xout_hbm[m * 128:(m + 1) * 128, t0:t0 + TC], in_=xo[:]), 16)
                    c.gset("at_xo", c.sX, ev_st)
                    c.gset(f"at_wo{m % 2}", c.sPE, c.sPE.n)
                c.gset("ao_ld", c.sPE, c.sPE.n)
            c.barrier()
        c.barrier()


def prep_attn_host(inp):
    return {
        "at_wqp": _panels(inp["L1_attn_Wq"], 1024, 1024),
        "at_wlp": _panels(inp["L1_attn_Wl"], 1024, 256),
        "at_wkp": _panels(inp["L1_attn_Wk"], 256, 1024),
        "at_wvp": _panels(inp["L1_attn_Wv"], 256, 1024),
        "at_wop": _panels(inp["L1_attn_Wo"], 1024, 1024),
        "atb_bq": _cols(inp["L1_attn_bq"]),
        "atb_bl": _cols(inp["L1_attn_bl"]),
        "atb_bk": _cols(inp["L1_attn_bk"]),
        "atb_bv": _cols(inp["L1_attn_bv"]),
        "atb_bo": _cols(inp["L1_attn_bo"]),
    }


def decl_attn(nc, cl):
    wqp = _wtens(nc, "at_wqp", [8, 128, 8 * 128], cl)
    wlp = _wtens(nc, "at_wlp", [2, 128, 8 * 128], cl)
    wkp = _wtens(nc, "at_wkp", [8, 128, 2 * 128], cl)
    wvp = _wtens(nc, "at_wvp", [8, 128, 2 * 128], cl)
    wop = _wtens(nc, "at_wop", [8, 128, 8 * 128], cl)
    return {
        "wqp": [wqp.ap()[m] for m in range(8)],
        "wlp": [wlp.ap()[m] for m in range(2)],
        "wkp": [wkp.ap()[m] for m in range(8)],
        "wvp": [wvp.ap()[m] for m in range(8)],
        "wop": [wop.ap()[m] for m in range(8)],
        "bq": nc.dram_tensor("atb_bq", [128, 8], F32, kind="ExternalInput").ap(),
        "bl": nc.dram_tensor("atb_bl", [128, 2], F32, kind="ExternalInput").ap(),
        "bk": nc.dram_tensor("atb_bk", [128, 8], F32, kind="ExternalInput").ap(),
        "bv": nc.dram_tensor("atb_bv", [128, 8], F32, kind="ExternalInput").ap(),
        "bo": nc.dram_tensor("atb_bo", [128, 8], F32, kind="ExternalInput").ap(),
    }


def decl_attn_scratch(nc):
    return {
        "qT": nc.dram_tensor("ats_qT", [D, S], BF16).ap(),
        "aoT": nc.dram_tensor("ats_aoT", [D, S], BF16).ap(),
        "m16": nc.dram_tensor("ats_m16", [8, 2], F32).ap(),
    }


def build_phase_attn():
    nc = bass.Bass("TRN2", target_bir_lowering=False)
    x_in = nc.dram_tensor("x_in", [D, S], F32, kind="ExternalInput")
    x_out = nc.dram_tensor("x_out", [D, S], F32, kind="ExternalOutput")
    xn_scr = nc.dram_tensor("xn_scr", [D, S], BF16)
    cl = []
    w = decl_attn(nc, cl)
    scr = decl_attn_scratch(nc)
    with contextlib.ExitStack() as es:
        c = Ctx(nc, es)
        emit_casts(c, cl)
        emit_ln(c, x_in.ap(), xn_scr.ap())
        emit_attn(c, w, x_in.ap(), xn_scr.ap(), x_out.ap(), scr)
        nc.sync.wait_ge(c.sX.h, c.sX.n)
        nc.sync.wait_ge(c.sW.h, c.sW.n)
    return nc


def _panels(wT, kdim, mdim):
    kt, mt = kdim // 128, mdim // 128
    a = wT.reshape(kt, 128, mt, 128).transpose(2, 1, 0, 3)
    return np.ascontiguousarray(a.reshape(mt, 128, kt * 128))


def _cols(v):
    n = v.shape[0] // 128
    return np.ascontiguousarray(v.reshape(n, 128).T)


def prep_ffn0_host(inp):
    return {
        "ffn_w1p": _panels(np.ascontiguousarray(inp["L0_ffn_w1_w"].T), 1024, 4096),
        "ffn_w2p": _panels(np.ascontiguousarray(inp["L0_ffn_w2_w"].T), 1024, 4096),
        "ffn_h1p": _panels(np.ascontiguousarray(inp["L0_ffn_h1_w"].T), 4096, 4096),
        "ffn_h2p": _panels(np.ascontiguousarray(inp["L0_ffn_h2_w"].T), 4096, 4096),
        "ffn_w3p": _panels(np.ascontiguousarray(inp["L0_ffn_w3_w"].T), 4096, 1024),
        "ffn_s1": _cols(inp["L0_ffn_w1_s"]),
        "ffn_s2": _cols(inp["L0_ffn_w2_s"]),
        "ffn_h1s": _cols(inp["L0_ffn_h1_s"]),
        "ffn_h2s": _cols(inp["L0_ffn_h2_s"]),
        "ffn_w3s": _cols(inp["L0_ffn_w3_s"]),
    }


def _wtens(nc, name, shape, cast_list):
    f = nc.dram_tensor(name, shape, F32, kind="ExternalInput")
    b = nc.dram_tensor(name + "_bf", shape, BF16)
    cast_list.append((b, f))
    return b


def decl_ffn0(nc, cl):
    d = {}
    d["w1p_t"] = _wtens(nc, "ffn_w1p", [32, 128, 8 * 128], cl)
    d["w2p_t"] = _wtens(nc, "ffn_w2p", [32, 128, 8 * 128], cl)
    d["h1p_t"] = _wtens(nc, "ffn_h1p", [32, 128, 32 * 128], cl)
    d["h2p_t"] = _wtens(nc, "ffn_h2p", [32, 128, 32 * 128], cl)
    d["w3p_t"] = _wtens(nc, "ffn_w3p", [8, 128, 32 * 128], cl)
    return {
        "w1p": [d["w1p_t"].ap()[m] for m in range(32)],
        "w2p": [d["w2p_t"].ap()[m] for m in range(32)],
        "h1p": [d["h1p_t"].ap()[m] for m in range(32)],
        "h2p": [d["h2p_t"].ap()[m] for m in range(32)],
        "w3p": [d["w3p_t"].ap()[m] for m in range(8)],
        "s1": nc.dram_tensor("ffn_s1", [128, 32], F32, kind="ExternalInput").ap(),
        "s2": nc.dram_tensor("ffn_s2", [128, 32], F32, kind="ExternalInput").ap(),
        "h1s": nc.dram_tensor("ffn_h1s", [128, 32], F32, kind="ExternalInput").ap(),
        "h2s": nc.dram_tensor("ffn_h2s", [128, 32], F32, kind="ExternalInput").ap(),
        "w3s": nc.dram_tensor("ffn_w3s", [128, 8], F32, kind="ExternalInput").ap(),
    }


def build_ln_only(dbg=False):
    nc = bass.Bass("TRN2", target_bir_lowering=False)
    x_in = nc.dram_tensor("x_in", [D, S], F32, kind="ExternalInput")
    xn_out = nc.dram_tensor("xn_out", [D, S], BF16, kind="ExternalOutput")
    dbg_stats = nc.dram_tensor("dbg_stats", [4, S], F32, kind="ExternalOutput").ap() if dbg else None
    with contextlib.ExitStack() as es:
        c = Ctx(nc, es)
        emit_ln(c, x_in.ap(), xn_out.ap(), dbg_stats=dbg_stats)
        nc.sync.wait_ge(c.sX.h, c.sX.n)
    return nc


def build_phase_ffn0():
    nc = bass.Bass("TRN2", target_bir_lowering=False)
    x_in = nc.dram_tensor("x_in", [D, S], F32, kind="ExternalInput")
    x_out = nc.dram_tensor("x_out", [D, S], F32, kind="ExternalOutput")
    xn_scr = nc.dram_tensor("xn_scr", [D, S], BF16)
    cl = []
    w = decl_ffn0(nc, cl)
    with contextlib.ExitStack() as es:
        c = Ctx(nc, es)
        emit_casts(c, cl)
        emit_ln(c, x_in.ap(), xn_scr.ap())
        emit_ffn0(c, w, x_in.ap(), xn_scr.ap(), x_out.ap())
        nc.sync.wait_ge(c.sX.h, c.sX.n)
        nc.sync.wait_ge(c.sW.h, c.sW.n)
    return nc


def emit_casts(c, cl):
    nc = c.nc
    for b, f in cl:
        ev = c.sPL.inc(nc.gpsimd.dma_start(out=b.ap(), in_=f.ap()), 16)
    nc.sync.wait_ge(c.sPL.h, ev)
    return ev


def build_fused(dbg=False):
    nc = bass.Bass("TRN2", target_bir_lowering=False)
    kd = "ExternalOutput" if dbg else "Internal"
    x0 = nc.dram_tensor("x_in", [D, S], F32, kind="ExternalInput")
    out = nc.dram_tensor("x_out", [D, S], F32, kind="ExternalOutput")
    xa = nc.dram_tensor("x_scr_a", [D, S], F32, kind=kd)
    xb = nc.dram_tensor("x_scr_b", [D, S], F32, kind=kd)
    xn = nc.dram_tensor("xn_scr", [D, S], BF16)
    cl = []
    w_ffn = decl_ffn0(nc, cl)
    w_mb = decl_mamba(nc, cl)
    w_moe = decl_moe(nc, cl)
    w_at = decl_attn(nc, cl)
    scr_mb = decl_mamba_scratch(nc)
    scr_at = decl_attn_scratch(nc)
    with contextlib.ExitStack() as es:
        c = Ctx(nc, es)
        emit_casts(c, cl)
        emit_ln(c, x0.ap(), xn.ap())
        emit_ffn0(c, w_ffn, x0.ap(), xn.ap(), xa.ap())
        emit_ln(c, xa.ap(), xn.ap())
        emit_mamba(c, w_mb, xa.ap(), xn.ap(), xb.ap(), scr_mb)
        emit_ln(c, xb.ap(), xn.ap())
        emit_ffn0(c, w_ffn, xb.ap(), xn.ap(), xa.ap())
        emit_ln(c, xa.ap(), xn.ap())
        emit_moe(c, w_moe, xa.ap(), xn.ap(), xb.ap())
        emit_ln(c, xb.ap(), xn.ap())
        emit_attn(c, w_at, xb.ap(), xn.ap(), xa.ap(), scr_at)
        emit_ln(c, xa.ap(), xn.ap())
        emit_moe(c, w_moe, xa.ap(), xn.ap(), out.ap())
        nc.sync.wait_ge(c.sX.h, c.sX.n)
        nc.sync.wait_ge(c.sW.h, c.sW.n)
    return nc


def kernel_fused(dbg=False, **inputs):
    import time as _t
    t0 = _t.time()
    nc = build_fused(dbg)
    w = {}
    w.update(prep_ffn0_host(inputs))
    w.update(prep_mamba_host(inputs))
    w.update(prep_moe_host(inputs))
    w.update(prep_attn_host(inputs))
    maps = []
    for cid in range(8):
        m = dict(w)
        m["x_in"] = np.ascontiguousarray(inputs["L0_attn_x"][cid].T)
        maps.append(m)
    res = run_bass_kernel_spmd(nc, maps, core_ids=list(range(8)))
    if dbg:
        return res
    return np.stack([res.results[cid]["x_out"].T for cid in range(8)]).astype(np.float32)


_PROG_CACHE = {}


def _phase_progs():
    if "ffn0" not in _PROG_CACHE:
        _PROG_CACHE["ffn0"] = build_phase_ffn0()
        _PROG_CACHE["mamba"] = build_phase_mamba()
        _PROG_CACHE["moe"] = build_phase_moe()
        _PROG_CACHE["attn"] = build_phase_attn()
    return _PROG_CACHE


def kernel(**inputs):
    return kernel_fused(**inputs)


def kernel_unfused(**inputs):
    progs = _phase_progs()
    w_ffn = prep_ffn0_host(inputs)
    w_mb = prep_mamba_host(inputs)
    w_moe = prep_moe_host(inputs)
    w_at = prep_attn_host(inputs)
    x = [np.ascontiguousarray(inputs["L0_attn_x"][cid].T) for cid in range(8)]

    def launch(prog, wmap):
        maps = []
        for cid in range(8):
            m = dict(wmap)
            m["x_in"] = x[cid]
            maps.append(m)
        res = run_bass_kernel_spmd(progs[prog], maps, core_ids=list(range(8)))
        for cid in range(8):
            x[cid] = np.ascontiguousarray(res.results[cid]["x_out"])

    launch("ffn0", w_ffn)
    launch("mamba", w_mb)
    launch("ffn0", w_ffn)
    launch("moe", w_moe)
    launch("attn", w_at)
    launch("moe", w_moe)
    return np.stack([x[cid].T for cid in range(8)]).astype(np.float32)


# version 05 (final)
# speedup vs baseline: 1.0735x; 1.0735x over previous
import contextlib
import numpy as np

import concourse.bass as bass
import concourse.mybir as mybir
from concourse.bass_utils import run_bass_kernel_spmd

F32 = mybir.dt.float32
BF16 = mybir.dt.bfloat16
U32 = mybir.dt.uint32
AF = mybir.ActivationFunctionType
ALU = mybir.AluOpType
AX = mybir.AxisListType

D = 1024
S = 4096
DI = 2048
NST = 16
DTR = 64
HID = 4096
HEADS = 16
DH = 64
LAT = 256
NEXP = 8
SCALE_ATTN = DH ** -0.5
TC = 1024
NMM = 512
NSUB = TC // NMM


_UID = [0]


def sbuf(nc, name, shape, dtype):
    _UID[0] += 1
    return nc.sbuf_tensor(f"{name}_{_UID[0]}", shape, dtype)


def psum(nc, name, shape, dtype=F32):
    _UID[0] += 1
    return nc.psum_tensor(f"{name}_{_UID[0]}", shape, dtype)


class Sem:

    def __init__(self, es, nc, name):
        self.es, self.nc, self.name = es, nc, name
        self.h = es.enter_context(nc.semaphore(name))
        self.n = 0
        self.uid = 0

    def inc(self, instr, k=1):
        instr.then_inc(self.h, k)
        self.n += k
        return self.n

    def rotate(self):
        if self.n < 12000:
            return
        self.uid += 1
        self.h = self.es.enter_context(self.nc.semaphore(f"{self.name}r{self.uid}"))
        self.n = 0


class Ctx:

    def __init__(self, nc, es):
        self.nc = nc
        self.es = es
        self.sW = Sem(es, nc, "sW")
        self.sX = Sem(es, nc, "sX")
        self.sPE = Sem(es, nc, "sPE")
        self.sACT = Sem(es, nc, "sACT")
        self.sDVE = Sem(es, nc, "sDVE")
        self.sPL = Sem(es, nc, "sPL")
        self.guards = {}
        self.ones_f32 = es.enter_context(sbuf(nc, "c_ones_f", [128, 128], F32))
        self.ones_bf = es.enter_context(sbuf(nc, "c_ones_b", [128, 128], BF16))
        self.ident_bf = es.enter_context(sbuf(nc, "c_id_b", [128, 128], BF16))
        nc.gpsimd.memset(self.ones_f32[:], 1.0)
        nc.gpsimd.memset(self.ones_bf[:], 1.0)
        self.cvals = es.enter_context(sbuf(nc, "c_vals", [128, 8], F32))
        for i, v in enumerate([1e-5, -0.5, SCALE_ATTN]):
            nc.gpsimd.memset(self.cvals[:, i:i + 1], v)
            nc.const_aps.register(F32, v, self.cvals[:, i:i + 1])
        ev = self.sPL.inc(nc.gpsimd.affine_select(
            self.ident_bf[:], self.ones_bf[:], [[-1, 128]], ALU.is_equal, 0.0,
            base=0, channel_multiplier=1))
        for eng in (nc.tensor, nc.vector, nc.scalar, nc.sync):
            eng.wait_ge(self.sPL.h, ev)

    def barrier(self):
        nc = self.nc
        for eng in (nc.tensor, nc.vector, nc.scalar, nc.gpsimd, nc.sync):
            for s in (self.sW, self.sX, self.sPE, self.sACT, self.sDVE, self.sPL):
                if s.n:
                    eng.wait_ge(s.h, s.n)
        for s in (self.sW, self.sX, self.sPE, self.sACT, self.sDVE, self.sPL):
            s.rotate()
        self.guards.clear()

    def gwait(self, eng, name):
        g = self.guards.get(name)
        if g is not None:
            eng.wait_ge(g[0], g[1])

    def gset(self, name, sem, ev):
        self.guards[name] = (sem.h, ev)


def pe_mark(c, instr):
    instr.then_inc(c.sPE.h, 1)
    c.sPE.n += 1
    return c.sPE.n


def bcol(t, col, ncols, width):
    return bass.AP(t, col, [[ncols, 128], [0, width]])


def emit_ln(c, x_hbm, xn_hbm, dbg_stats=None):
    nc = c.nc
    with contextlib.ExitStack() as es:
        xt = [es.enter_context(sbuf(nc, f"ln_x{i}", [128, TC], F32)) for i in range(8)]
        xb = [es.enter_context(sbuf(nc, f"ln_b{i}", [128, TC], BF16)) for i in range(8)]
        xq = [es.enter_context(sbuf(nc, f"ln_q{i}", [128, TC], BF16)) for i in range(8)]
        mean_sb = es.enter_context(sbuf(nc, "ln_mean", [1, TC], F32))
        var_sb = es.enter_context(sbuf(nc, "ln_var", [1, TC], F32))
        rstd_sb = es.enter_context(sbuf(nc, "ln_rstd", [1, TC], F32))
        nmr_sb = es.enter_context(sbuf(nc, "ln_nmr", [1, TC], F32))
        t1 = es.enter_context(sbuf(nc, "ln_t1", [128, TC], F32))
        xn = es.enter_context(sbuf(nc, "ln_xn", [128, TC], BF16))
        sum_ps = es.enter_context(psum(nc, "ln_sum_ps", [1, TC], F32))
        ssq_ps = es.enter_context(psum(nc, "ln_ssq_ps", [1, TC], F32))
        brstd_ps = es.enter_context(psum(nc, "ln_brstd_ps", [128, TC], F32))
        bnmr_ps = es.enter_context(psum(nc, "ln_bnmr_ps", [128, TC], F32))
        for ch in range(S // TC):
            t0 = ch * TC
            ev_ld = []
            for d in range(8):
                c.gwait(nc.sync, "ln_xt")
                ev_ld.append(c.sX.inc(nc.sync.dma_start(
                    out=xt[d][:], in_=x_hbm[d * 128:(d + 1) * 128, t0:t0 + TC]), 16))
            ev_sq = []
            for d in range(8):
                nc.scalar.wait_ge(c.sX.h, ev_ld[d])
                c.gwait(nc.scalar, "ln_xq")
                nc.scalar.activation(xb[d][:], xt[d][:], AF.Identity)
                ev_sq.append(c.sACT.inc(nc.scalar.activation(xq[d][:], xt[d][:], AF.Square)))
            c.gwait(nc.tensor, "ln_stats_ps")
            for d in range(8):
                nc.tensor.wait_ge(c.sACT.h, ev_sq[d])
                for n2 in range(NSUB):
                    sl = slice(n2 * NMM, (n2 + 1) * NMM)
                    nc.tensor.matmul(sum_ps[:, sl], c.ones_bf[:, 0:1], xb[d][:, sl],
                                     start=(d == 0), stop=(d == 7))
            for d in range(8):
                for n2 in range(NSUB):
                    sl = slice(n2 * NMM, (n2 + 1) * NMM)
                    mm = nc.tensor.matmul(ssq_ps[:, sl], c.ones_bf[:, 0:1], xq[d][:, sl],
                                          start=(d == 0), stop=(d == 7))
            ev_stats = pe_mark(c, mm)
            c.gset("ln_xq", c.sPE, ev_stats)
            nc.vector.wait_ge(c.sPE.h, ev_stats)
            nc.vector.tensor_scalar(mean_sb[:], sum_ps[:], 1.0 / D, None, ALU.mult)
            nc.vector.tensor_scalar(var_sb[:], ssq_ps[:], 1.0 / D, None, ALU.mult)
            nc.vector.tensor_mul(t1[0:1, :], mean_sb[:], mean_sb[:])
            ev_var = c.sDVE.inc(nc.vector.tensor_sub(var_sb[:], var_sb[:], t1[0:1, :]))
            nc.scalar.wait_ge(c.sDVE.h, ev_var)
            nc.scalar.activation(rstd_sb[:], var_sb[:], AF.Ln, bias=1e-5)
            ev_rstd = c.sACT.inc(nc.scalar.activation(rstd_sb[:], rstd_sb[:], AF.Exp, scale=-0.5))
            nc.vector.wait_ge(c.sACT.h, ev_rstd)
            ev_nmr = c.sDVE.inc(nc.vector.scalar_tensor_tensor(
                nmr_sb[:], mean_sb[:], -1.0, rstd_sb[:], ALU.mult, ALU.mult))
            c.gset("ln_stats_ps", c.sDVE, ev_nmr)
            if dbg_stats is not None:
                nc.sync.wait_ge(c.sDVE.h, ev_nmr)
                c.sX.inc(nc.sync.dma_start(out=dbg_stats[0:1, t0:t0 + TC], in_=mean_sb[:]), 16)
                c.sX.inc(nc.sync.dma_start(out=dbg_stats[1:2, t0:t0 + TC], in_=var_sb[:]), 16)
                c.sX.inc(nc.sync.dma_start(out=dbg_stats[2:3, t0:t0 + TC], in_=rstd_sb[:]), 16)
                c.sX.inc(nc.sync.dma_start(out=dbg_stats[3:4, t0:t0 + TC], in_=nmr_sb[:]), 16)
            c.gwait(nc.tensor, "ln_bcast_ps")
            nc.tensor.wait_ge(c.sDVE.h, ev_nmr)
            for n2 in range(NSUB):
                sl = slice(n2 * NMM, (n2 + 1) * NMM)
                nc.tensor.matmul(brstd_ps[:, sl], c.ones_f32[0:1, :], rstd_sb[:, sl],
                                 start=True, stop=True)
                mm = nc.tensor.matmul(bnmr_ps[:, sl], c.ones_f32[0:1, :], nmr_sb[:, sl],
                                      start=True, stop=True)
            ev_bcast = pe_mark(c, mm)
            nc.vector.wait_ge(c.sPE.h, ev_bcast)
            for d in range(8):
                c.gwait(nc.vector, "ln_xn")
                nc.vector.tensor_mul(t1[:], xt[d][:], brstd_ps[:])
                ev_xn = c.sDVE.inc(nc.vector.tensor_add(xn[:], t1[:], bnmr_ps[:]))
                nc.sync.wait_ge(c.sDVE.h, ev_xn)
                ev_st = c.sX.inc(nc.sync.dma_start(
                    out=xn_hbm[d * 128:(d + 1) * 128, t0:t0 + TC], in_=xn[:]), 16)
                c.gset("ln_xn", c.sX, ev_st)
            c.gset("ln_bcast_ps", c.sDVE, c.sDVE.n)
            c.gset("ln_xt", c.sDVE, c.sDVE.n)
        c.barrier()


def emit_ffn0(c, w, x_hbm, xn_hbm, xout_hbm):
    nc = c.nc
    with contextlib.ExitStack() as es:
        s1t = es.enter_context(sbuf(nc, "f_s1", [128, 32], F32))
        s2t = es.enter_context(sbuf(nc, "f_s2", [128, 32], F32))
        h1st = es.enter_context(sbuf(nc, "f_h1s", [128, 32], F32))
        h2st = es.enter_context(sbuf(nc, "f_h2s", [128, 32], F32))
        w3st = es.enter_context(sbuf(nc, "f_w3s", [128, 8], F32))
        w3h = es.enter_context(sbuf(nc, "f_w3h", [128, 8], F32))
        ev = 0
        for dst, src in ((s1t, w["s1"]), (s2t, w["s2"]), (h1st, w["h1s"]),
                         (h2st, w["h2s"]), (w3st, w["w3s"])):
            ev = c.sX.inc(nc.sync.dma_start(out=dst[:], in_=src), 16)
        nc.vector.wait_ge(c.sX.h, ev)
        ev_w3h = c.sDVE.inc(nc.vector.tensor_scalar(w3h[:], w3st[:], 0.5, None, ALU.mult))
        nc.vector.wait_ge(c.sDVE.h, ev_w3h)
        nc.scalar.wait_ge(c.sX.h, ev)

        xnt = [es.enter_context(sbuf(nc, f"f_xn{i}", [128, TC], BF16)) for i in range(8)]
        hbuf = [es.enter_context(sbuf(nc, f"f_h{i}", [128, TC], BF16)) for i in range(32)]
        mbuf = [es.enter_context(sbuf(nc, f"f_m{i}", [128, TC], BF16)) for i in range(32)]
        sg = es.enter_context(sbuf(nc, "f_sg", [128, TC], BF16))
        xr = es.enter_context(sbuf(nc, "f_xr", [128, TC], F32))
        xo = es.enter_context(sbuf(nc, "f_xo", [128, TC], F32))
        wpan = [es.enter_context(sbuf(nc, f"f_wp{i}", [128, 32 * 128], BF16)) for i in range(2)]
        wg = [es.enter_context(sbuf(nc, f"f_wg{i}", [128, 8 * 128], BF16)) for i in range(4)]
        ps = [es.enter_context(psum(nc, f"f_ps{i}", [128, NMM], F32)) for i in range(8)]

        for ch in range(S // TC):
            t0 = ch * TC
            ev_xn = []
            for d in range(8):
                c.gwait(nc.sync, "f_xnt")
                ev_xn.append(c.sX.inc(nc.sync.dma_start(
                    out=xnt[d][:], in_=xn_hbm[d * 128:(d + 1) * 128, t0:t0 + TC]), 16))

            for m in range(32):
                b1 = wg[m % 2]
                b2 = wg[2 + m % 2]
                c.gwait(nc.sync, f"f_wg{m % 2}")
                ev_p1 = c.sW.inc(nc.sync.dma_start(out=b1[:], in_=w["w1p"][m]), 16)
                c.gwait(nc.sync, f"f_wg{2 + m % 2}")
                ev_p2 = c.sW.inc(nc.sync.dma_start(out=b2[:], in_=w["w2p"][m]), 16)
                nc.tensor.wait_ge(c.sW.h, ev_p2)
                nc.tensor.wait_ge(c.sX.h, ev_xn[7])
                for n2 in range(NSUB):
                    s0, s1_ = 2 * n2, 2 * n2 + 1
                    c.gwait(nc.tensor, f"f_ps{s0}")
                    c.gwait(nc.tensor, f"f_ps{s1_}")
                    for k in range(8):
                        nc.tensor.matmul(ps[s0][:], b1[:, k * 128:(k + 1) * 128],
                                         xnt[k][:, n2 * NMM:(n2 + 1) * NMM],
                                         start=(k == 0), stop=(k == 7))
                    for k in range(8):
                        mm = nc.tensor.matmul(ps[s1_][:], b2[:, k * 128:(k + 1) * 128],
                                              xnt[k][:, n2 * NMM:(n2 + 1) * NMM],
                                              start=(k == 0), stop=(k == 7))
                    ev_mm = pe_mark(c, mm)
                    nc.scalar.wait_ge(c.sPE.h, ev_mm)
                    c.gwait(nc.scalar, "f_sg")
                    ev_a = c.sACT.inc(nc.scalar.activation(
                        sg[:, n2 * NMM:(n2 + 1) * NMM], ps[s0][:], AF.Silu,
                        scale=s1t[:, m:m + 1]))
                    nc.vector.wait_ge(c.sACT.h, ev_a)
                    ev_v = c.sDVE.inc(nc.vector.scalar_tensor_tensor(
                        hbuf[m][:, n2 * NMM:(n2 + 1) * NMM], ps[s1_][:],
                        s2t[:, m:m + 1], sg[:, n2 * NMM:(n2 + 1) * NMM],
                        ALU.mult, ALU.mult))
                    c.gset(f"f_ps{s0}", c.sACT, ev_a)
                    c.gset(f"f_ps{s1_}", c.sDVE, ev_v)
                    c.gset("f_sg", c.sDVE, ev_v)
                c.gset(f"f_wg{m % 2}", c.sPE, c.sPE.n)
                c.gset(f"f_wg{2 + m % 2}", c.sPE, c.sPE.n)
            c.gset("f_xnt", c.sPE, c.sPE.n)
            ev_h_done = (c.sDVE.h, c.sDVE.n)

            def stage_hh(panel_list, src_bufs, dst_bufs, scale_t, ev_src):
                for m in range(32):
                    pb = wpan[m % 2]
                    c.gwait(nc.sync, f"f_wpan{m % 2}")
                    ev_p = c.sW.inc(nc.sync.dma_start(out=pb[:], in_=panel_list[m]), 16)
                    nc.tensor.wait_ge(c.sW.h, ev_p)
                    nc.tensor.wait_ge(ev_src[0], ev_src[1])
                    for n2 in range(NSUB):
                        slot = (m * NSUB + n2) % 8
                        c.gwait(nc.tensor, f"f_ps{slot}")
                        for k in range(32):
                            mm = nc.tensor.matmul(ps[slot][:], pb[:, k * 128:(k + 1) * 128],
                                                  src_bufs[k][:, n2 * NMM:(n2 + 1) * NMM],
                                                  start=(k == 0), stop=(k == 31))
                        ev_mm = pe_mark(c, mm)
                        nc.scalar.wait_ge(c.sPE.h, ev_mm)
                        ev_a = c.sACT.inc(nc.scalar.activation(
                            dst_bufs[m][:, n2 * NMM:(n2 + 1) * NMM], ps[slot][:],
                            AF.Gelu, scale=scale_t[:, m:m + 1]))
                        c.gset(f"f_ps{slot}", c.sACT, ev_a)
                    c.gset(f"f_wpan{m % 2}", c.sPE, c.sPE.n)
                return (c.sACT.h, c.sACT.n)

            ev_m_done = stage_hh(w["h1p"], hbuf, mbuf, h1st, ev_h_done)
            ev_h2_done = stage_hh(w["h2p"], mbuf, hbuf, h2st, ev_m_done)

            for m in range(8):
                pb = wpan[m % 2]
                c.gwait(nc.sync, f"f_wpan{m % 2}")
                ev_p = c.sW.inc(nc.sync.dma_start(out=pb[:], in_=w["w3p"][m]), 16)
                c.gwait(nc.sync, "f_xr")
                ev_xr = c.sX.inc(nc.sync.dma_start(
                    out=xr[:], in_=x_hbm[m * 128:(m + 1) * 128, t0:t0 + TC]), 16)
                nc.tensor.wait_ge(c.sW.h, ev_p)
                nc.tensor.wait_ge(ev_h2_done[0], ev_h2_done[1])
                for n2 in range(NSUB):
                    slot = (m * NSUB + n2) % 8
                    c.gwait(nc.tensor, f"f_ps{slot}")
                    for k in range(32):
                        mm = nc.tensor.matmul(ps[slot][:], pb[:, k * 128:(k + 1) * 128],
                                              hbuf[k][:, n2 * NMM:(n2 + 1) * NMM],
                                              start=(k == 0), stop=(k == 31))
                    ev_mm = pe_mark(c, mm)
                    nc.vector.wait_ge(c.sPE.h, ev_mm)
                    nc.vector.wait_ge(c.sX.h, ev_xr)
                    c.gwait(nc.vector, "f_xo")
                    ev_v = c.sDVE.inc(nc.vector.scalar_tensor_tensor(
                        xo[:, n2 * NMM:(n2 + 1) * NMM], ps[slot][:],
                        w3h[:, m:m + 1], xr[:, n2 * NMM:(n2 + 1) * NMM],
                        ALU.mult, ALU.add))
                    c.gset(f"f_ps{slot}", c.sDVE, ev_v)
                c.gset("f_xr", c.sDVE, c.sDVE.n)
                nc.sync.wait_ge(c.sDVE.h, c.sDVE.n)
                ev_st = c.sX.inc(nc.sync.dma_start(
                    out=xout_hbm[m * 128:(m + 1) * 128, t0:t0 + TC], in_=xo[:]), 16)
                c.gset("f_xo", c.sX, ev_st)
                c.gset(f"f_wpan{m % 2}", c.sPE, c.sPE.n)
        c.barrier()


def emit_mamba(c, w, x_hbm, xn_hbm, xout_hbm, scr, passes=(1, 1, 1)):
    nc = c.nc
    NDT = DI // 128
    with contextlib.ExitStack() as es:
        wx_sb = es.enter_context(sbuf(nc, "ms_wx", [128, 16 * 96], BF16))
        wdt_sb = es.enter_context(sbuf(nc, "ms_wdt", [64, DI], BF16))
        cw = es.enter_context(sbuf(nc, "ms_cw", [128, 16 * 4], F32))
        cb = es.enter_context(sbuf(nc, "ms_cb", [128, 16], F32))
        bdt = es.enter_context(sbuf(nc, "ms_bdt", [128, 16], F32))
        dcol = es.enter_context(sbuf(nc, "mb_d", [128, 16], F32))
        acols = es.enter_context(sbuf(nc, "ms_acols", [128, 256], F32))
        negA = es.enter_context(sbuf(nc, "mb_negA", [128, 256], F32))
        sel = es.enter_context(sbuf(nc, "mb_sel", [64, 32 * 128], BF16))
        bc_sb = es.enter_context(sbuf(nc, "mb_bc", [64, S], BF16))
        blo = es.enter_context(sbuf(nc, "mb_blo", [32, TC], BF16))
        btmp = es.enter_context(sbuf(nc, "mb_btmp", [32, TC], F32))
        hcarry = [es.enter_context(sbuf(nc, f"mb_hc{i}", [128, 256], F32)) for i in range(1)]
        hcarry = hcarry[0]
        halo = es.enter_context(sbuf(nc, "mb_halo", [128, 16 * 3], F32))
        ev = 0
        ev = c.sW.inc(nc.sync.dma_start(out=wx_sb[:], in_=w["wx"]), 16)
        ev = c.sW.inc(nc.sync.dma_start(out=wdt_sb[:], in_=w["wdt"]), 16)
        ev_w = 0
        for dst, src in ((cw, w["cw"]), (cb, w["cb"]), (bdt, w["bdt"]),
                         (dcol, w["dcol"]), (acols, w["acols"])):
            ev_w = c.sX.inc(nc.sync.dma_start(out=dst[:], in_=src), 16)
        nc.scalar.wait_ge(c.sX.h, ev_w)
        ev_a = c.sACT.inc(nc.scalar.activation(negA[:], acols[:], AF.Exp))
        nc.vector.wait_ge(c.sACT.h, ev_a)
        ev_na = c.sDVE.inc(nc.vector.tensor_scalar(negA[:], negA[:], -1.0, None, ALU.mult))
        nc.scalar.wait_ge(c.sDVE.h, ev_na)
        nc.vector.wait_ge(c.sX.h, ev_w)
        nc.gpsimd.memset(sel[:], 1.0)
        ev_sel = c.sPL.inc(nc.gpsimd.affine_select(
            sel[0:32, :], sel[0:32, :], [[-1, 32], [0, 128]], ALU.is_equal, 0.0,
            base=0, channel_multiplier=1))
        nc.sync.wait_ge(c.sPL.h, ev_sel)
        ev_sel = c.sX.inc(nc.sync.dma_start(out=sel[32:64, :], in_=sel[0:32, :]), 16)
        nc.tensor.wait_ge(c.sX.h, ev_sel)
        nc.vector.memset(hcarry[:], 0.0)
        ev_hz = c.sDVE.inc(nc.vector.memset(halo[:], 0.0))
        nc.vector.wait_ge(c.sDVE.h, ev_hz)

        TCA = 512
        NSUBA = TCA // NMM
        if not passes[0]:
            return
        c.barrier()
        with contextlib.ExitStack() as esA:
            xin = [esA.enter_context(sbuf(nc, f"mb_xi{i}", [128, TCA + 3], F32)) for i in range(2)]
            u32 = [esA.enter_context(sbuf(nc, f"mb_u{i}", [128, TCA], F32)) for i in range(NDT)]
            ubf = [esA.enter_context(sbuf(nc, f"mb_ub{i}", [128, TCA], BF16)) for i in range(NDT)]
            resb = [esA.enter_context(sbuf(nc, f"mb_rs{i}", [128, TCA], BF16)) for i in range(2)]
            dtbf = esA.enter_context(sbuf(nc, "mb_dtb", [64, TCA], BF16))
            drot = [esA.enter_context(sbuf(nc, f"mb_dr{i}", [128, TCA], F32)) for i in range(2)]
            durot = [esA.enter_context(sbuf(nc, f"mb_du{i}", [128, TCA], F32)) for i in range(2)]
            tt1 = esA.enter_context(sbuf(nc, "mb_t1", [128, TCA], F32))
            tt2 = esA.enter_context(sbuf(nc, "mb_t2", [128, TCA], F32))
            xnt = [esA.enter_context(sbuf(nc, f"mb_xn{i}", [128, TCA], BF16)) for i in range(8)]
            wpan = [esA.enter_context(sbuf(nc, f"mb_wp{i}", [128, 8 * 128], BF16)) for i in range(2)]
            ps = [esA.enter_context(psum(nc, f"mb_ps{i}", [128, NMM], F32)) for i in range(4)]
            psd = [esA.enter_context(psum(nc, f"mb_psd{i}", [64, NMM], F32)) for i in range(2)]
            psb = [esA.enter_context(psum(nc, f"mb_psb{i}", [32, NMM], F32)) for i in range(2)]
            for ch in range(S // TCA):
                t0 = ch * TCA
                ev_xn = 0
                for d in range(8):
                    c.gwait(nc.sync, "mb_xnt")
                    ev_xn = c.sX.inc(nc.sync.dma_start(
                        out=xnt[d][:], in_=xn_hbm[d * 128:(d + 1) * 128, t0:t0 + TCA]), 16)
                for m in range(32):
                    pb = wpan[m % 2]
                    c.gwait(nc.sync, f"mb_wp{m % 2}")
                    ev_p = c.sW.inc(nc.sync.dma_start(out=pb[:], in_=w["winp"][m]), 16)
                    nc.tensor.wait_ge(c.sW.h, ev_p)
                    nc.tensor.wait_ge(c.sX.h, ev_xn)
                    for n2 in range(NSUBA):
                        slot = (m * NSUBA + n2) % 4
                        c.gwait(nc.tensor, f"mb_ps{slot}")
                        for k in range(8):
                            mm = nc.tensor.matmul(ps[slot][:], pb[:, k * 128:(k + 1) * 128],
                                                  xnt[k][:, n2 * NMM:(n2 + 1) * NMM],
                                                  start=(k == 0), stop=(k == 7))
                        ev_mm = pe_mark(c, mm)
                        nc.scalar.wait_ge(c.sPE.h, ev_mm)
                        if m < 16:
                            xb = xin[m % 2]
                            if n2 == 0:
                                c.gwait(nc.scalar, f"mb_xi{m % 2}")
                                nc.scalar.wait_ge(c.sDVE.h, c.guards.get(f"mb_halo{m}", (c.sDVE.h, 0))[1])
                                ev_hc = c.sACT.inc(nc.scalar.copy(xb[:, 0:3], halo[:, m * 3:m * 3 + 3]))
                            ev_e = c.sACT.inc(nc.scalar.copy(
                                xb[:, 3 + n2 * NMM:3 + (n2 + 1) * NMM], ps[slot][:]))
                            c.gset(f"mb_ps{slot}", c.sACT, ev_e)
                        else:
                            rb = resb[m % 2]
                            c.gwait(nc.scalar, f"mb_rs{m % 2}") if n2 == 0 else None
                            ev_e = c.sACT.inc(nc.scalar.activation(
                                rb[:, n2 * NMM:(n2 + 1) * NMM], ps[slot][:], AF.Silu))
                            c.gset(f"mb_ps{slot}", c.sACT, ev_e)
                    if m < 16:
                        dt = m
                        xb = xin[m % 2]
                        nc.vector.wait_ge(c.sACT.h, ev_e)
                        ev_hs = c.sDVE.inc(nc.vector.tensor_copy(
                            halo[:, dt * 3:dt * 3 + 3], xb[:, TCA:TCA + 3]))
                        c.gset(f"mb_halo{dt}", c.sDVE, ev_hs)
                        ua = u32[dt]
                        c.gwait(nc.vector, f"mb_u{dt}")
                        nc.vector.tensor_scalar(ua[:], xb[:, 0:TCA], cw[:, dt * 4:dt * 4 + 1],
                                                cb[:, dt:dt + 1], ALU.mult, ALU.add)
                        for kk in range(1, 4):
                            nc.vector.scalar_tensor_tensor(
                                ua[:], xb[:, kk:kk + TCA], cw[:, dt * 4 + kk:dt * 4 + kk + 1],
                                ua[:], ALU.mult, ALU.add)
                        ev_cv = c.sDVE.inc(nc.vector.tensor_copy(tt1[0:1, 0:1], ua[0:1, 0:1]))
                        c.gset(f"mb_xi{m % 2}", c.sDVE, ev_cv)
                        nc.scalar.wait_ge(c.sDVE.h, ev_cv)
                        ev_su = c.sACT.inc(nc.scalar.activation(ua[:], ua[:], AF.Silu))
                        nc.vector.wait_ge(c.sACT.h, ev_su)
                        c.gwait(nc.vector, f"mb_ubf{dt}")
                        ev_ub = c.sDVE.inc(nc.vector.tensor_copy(ubf[dt][:], ua[:]))
                    else:
                        rt = m - 16
                        nc.sync.wait_ge(c.sACT.h, ev_e)
                        ev_st = c.sX.inc(nc.sync.dma_start(
                            out=scr["res"][rt * 128:(rt + 1) * 128, t0:t0 + TCA],
                            in_=resb[m % 2][:]), 16)
                        c.gset(f"mb_rs{m % 2}", c.sX, ev_st)
                    c.gset(f"mb_wp{m % 2}", c.sPE, c.sPE.n)
                c.gset("mb_xnt", c.sPE, c.sPE.n)
                ev_ub_all = c.sDVE.n

                nc.tensor.wait_ge(c.sDVE.h, ev_ub_all)
                for n2 in range(NSUBA):
                    c.gwait(nc.tensor, f"mb_psd{n2}")
                    c.gwait(nc.tensor, f"mb_psb{n2}")
                    for k in range(NDT):
                        nc.tensor.matmul(psd[n2][:], wx_sb[:, k * 96:k * 96 + 64],
                                         ubf[k][:, n2 * NMM:(n2 + 1) * NMM],
                                         start=(k == 0), stop=(k == NDT - 1))
                    for k in range(NDT):
                        mm = nc.tensor.matmul(psb[n2][:], wx_sb[:, k * 96 + 64:k * 96 + 96],
                                              ubf[k][:, n2 * NMM:(n2 + 1) * NMM],
                                              start=(k == 0), stop=(k == NDT - 1))
                    ev_mm = pe_mark(c, mm)
                    nc.scalar.wait_ge(c.sPE.h, ev_mm)
                    c.gwait(nc.scalar, "mb_dtb") if n2 == 0 else None
                    ev_dt = c.sACT.inc(nc.scalar.copy(dtbf[:, n2 * NMM:(n2 + 1) * NMM], psd[n2][:]))
                    c.gset(f"mb_psd{n2}", c.sACT, ev_dt)
                    sl_t = slice(t0 + n2 * NMM, t0 + (n2 + 1) * NMM)
                    sl_l = slice(n2 * NMM, (n2 + 1) * NMM)
                    nc.scalar.wait_ge(c.sPE.h, ev_mm)
                    c.gwait(nc.scalar, "mb_bclo")
                    ev_hi = c.sACT.inc(nc.scalar.activation(
                        bc_sb[0:32, sl_t], psb[n2][:], AF.Identity))
                    nc.vector.wait_ge(c.sACT.h, ev_hi)
                    ev_tm = c.sDVE.inc(nc.vector.tensor_sub(
                        btmp[:, sl_l], psb[n2][:], bc_sb[0:32, sl_t]))
                    c.gset(f"mb_psb{n2}", c.sDVE, ev_tm)
                    nc.scalar.wait_ge(c.sDVE.h, ev_tm)
                    ev_lo = c.sACT.inc(nc.scalar.activation(
                        blo[:, sl_l], btmp[:, sl_l], AF.Identity))
                    nc.sync.wait_ge(c.sACT.h, ev_lo)
                    ev_bc = c.sX.inc(nc.sync.dma_start(
                        out=bc_sb[32:64, sl_t], in_=blo[:, sl_l]), 16)
                    c.gset("mb_bclo", c.sX, ev_bc)
                ev_dt_all = c.sACT.n
                for _dt in range(NDT):
                    c.gset(f"mb_ubf{_dt}", c.sPE, c.sPE.n)

                nc.tensor.wait_ge(c.sACT.h, ev_dt_all)
                for dt in range(NDT):
                    dr = drot[dt % 2]
                    du = durot[dt % 2]
                    for n2 in range(NSUBA):
                        slot = (dt * NSUBA + n2) % 4
                        c.gwait(nc.tensor, f"mb_ps{slot}")
                        mm = nc.tensor.matmul(ps[slot][:], wdt_sb[:, dt * 128:(dt + 1) * 128],
                                              dtbf[:, n2 * NMM:(n2 + 1) * NMM],
                                              start=True, stop=True)
                        ev_mm = pe_mark(c, mm)
                        sl = slice(n2 * NMM, (n2 + 1) * NMM)
                        nc.scalar.wait_ge(c.sPE.h, ev_mm)
                        if n2 == 0:
                            c.gwait(nc.scalar, f"mb_dr{dt % 2}")
                        c.gwait(nc.scalar, "mb_tt")
                        nc.scalar.activation(tt1[:, sl], ps[slot][:], AF.Abs, bias=bdt[:, dt:dt + 1])
                        nc.scalar.activation(tt1[:, sl], tt1[:, sl], AF.Exp, scale=-1.0)
                        nc.scalar.activation(tt1[:, sl], tt1[:, sl], AF.Ln, bias=1.0)
                        ev_r = c.sACT.inc(nc.scalar.activation(
                            tt2[:, sl], ps[slot][:], AF.Relu, bias=bdt[:, dt:dt + 1]))
                        c.gset(f"mb_ps{slot}", c.sACT, ev_r)
                        nc.vector.wait_ge(c.sACT.h, ev_r)
                        ev_add = c.sDVE.inc(nc.vector.tensor_add(dr[:, sl], tt1[:, sl], tt2[:, sl]))
                        c.gset("mb_tt", c.sDVE, ev_add)
                    nc.vector.scalar_tensor_tensor(du[:], u32[dt][:], 1.0, dr[:], ALU.mult, ALU.mult) if False else None
                    nc.vector.tensor_mul(du[:], dr[:], u32[dt][:])
                    ev_ud = c.sDVE.inc(nc.vector.tensor_scalar(
                        u32[dt][:], u32[dt][:], dcol[:, dt:dt + 1], None, ALU.mult))
                    nc.sync.wait_ge(c.sDVE.h, ev_ud)
                    c.sX.inc(nc.sync.dma_start(
                        out=scr["delta"][dt * 128:(dt + 1) * 128, t0:t0 + TCA], in_=dr[:]), 16)
                    c.sX.inc(nc.sync.dma_start(
                        out=scr["du"][dt * 128:(dt + 1) * 128, t0:t0 + TCA], in_=du[:]), 16)
                    ev_st = c.sX.inc(nc.sync.dma_start(
                        out=scr["uD"][dt * 128:(dt + 1) * 128, t0:t0 + TCA], in_=u32[dt][:]), 16)
                    c.gset(f"mb_dr{dt % 2}", c.sX, ev_st)
                    c.gset(f"mb_u{dt}", c.sX, ev_st)

        if not passes[1]:
            return
        c.barrier()
        TQ = 1024
        NQ = S // TQ
        with contextlib.ExitStack() as esB:
            dl = [esB.enter_context(sbuf(nc, f"sc_d{i}", [128, S], F32)) for i in range(2)]
            dul = [esB.enter_context(sbuf(nc, f"sc_u{i}", [128, S], F32)) for i in range(2)]
            yb = [esB.enter_context(sbuf(nc, f"sc_y{i}", [128, S], F32)) for i in range(2)]
            a_sb = esB.enter_context(sbuf(nc, "sc_a", [128, TQ], F32))
            b_sb = esB.enter_context(sbuf(nc, "sc_b", [128, TQ], F32))
            h_sb = esB.enter_context(sbuf(nc, "sc_h", [128, TQ], F32))
            z_sb = esB.enter_context(sbuf(nc, "sc_z", [128, TQ], F32))
            psB = [esB.enter_context(psum(nc, f"sc_pb{i}", [128, TQ], F32)) for i in range(2)]
            psC = [esB.enter_context(psum(nc, f"sc_pc{i}", [128, TQ], F32)) for i in range(2)]
            for g in range(NDT // 2):
                dts = (2 * g, 2 * g + 1)
                ev_ld = 0
                for i, dt in enumerate(dts):
                    c.gwait(nc.sync, f"sc_ld{i}")
                    c.sX.inc(nc.sync.dma_start(
                        out=dl[i][:], in_=scr["delta"][dt * 128:(dt + 1) * 128, :]), 16)
                    ev_ld = c.sX.inc(nc.sync.dma_start(
                        out=dul[i][:], in_=scr["du"][dt * 128:(dt + 1) * 128, :]), 16)
                for n in range(NST):
                    for q in range(NQ):
                        qsl = slice(q * TQ, (q + 1) * TQ)
                        pb_ = psB[q % 2]
                        pc_ = psC[q % 2]
                        c.gwait(nc.tensor, f"sc_pb{q % 2}")
                        c.gwait(nc.tensor, f"sc_pc{q % 2}")
                        for n2 in range(TQ // NMM):
                            ssl = slice(q * TQ + n2 * NMM, q * TQ + (n2 + 1) * NMM)
                            nc.tensor.matmul(pb_[:, n2 * NMM:(n2 + 1) * NMM],
                                             sel[:, n * 128:(n + 1) * 128],
                                             bc_sb[:, ssl], start=True, stop=True)
                            mm = nc.tensor.matmul(pc_[:, n2 * NMM:(n2 + 1) * NMM],
                                                  sel[:, (16 + n) * 128:(16 + n + 1) * 128],
                                                  bc_sb[:, ssl], start=True, stop=True)
                        ev_bc = pe_mark(c, mm)
                        for i, dt in enumerate(dts):
                            if g == 0 and n == 0 and q == 0 and i == 0:
                                nc.scalar.wait_ge(c.sX.h, ev_ld)
                            nc.scalar.wait_ge(c.sX.h, ev_ld)
                            c.gwait(nc.scalar, "sc_a")
                            ev_a = c.sACT.inc(nc.scalar.activation(
                                a_sb[:], dl[i][:, qsl], AF.Exp,
                                scale=negA[:, dt * 16 + n:dt * 16 + n + 1]))
                            nc.vector.wait_ge(c.sACT.h, ev_a)
                            nc.vector.wait_ge(c.sPE.h, ev_bc)
                            c.gwait(nc.vector, "sc_b")
                            ev_b = c.sDVE.inc(nc.vector.tensor_mul(b_sb[:], dul[i][:, qsl], pb_[:]))
                            nc.vector.wait_ge(c.sDVE.h, ev_b)
                            c.gwait(nc.vector, "sc_hcar")
                            ev_sc = c.sDVE.inc(nc.vector.tensor_tensor_scan(
                                h_sb[:], a_sb[:], b_sb[:],
                                hcarry[:, dt * 16 + n:dt * 16 + n + 1], ALU.mult, ALU.add))
                            nc.vector.wait_ge(c.sDVE.h, ev_sc)
                            ev_hc = c.sDVE.inc(nc.vector.tensor_copy(
                                hcarry[:, dt * 16 + n:dt * 16 + n + 1], h_sb[:, TQ - 1:TQ]))
                            c.gset("sc_hcar", c.sDVE, ev_hc)
                            c.gset("sc_a", c.sDVE, ev_hc)
                            c.gset("sc_b", c.sDVE, ev_hc)
                            nc.vector.tensor_mul(z_sb[:], h_sb[:], pc_[:])
                            if n == 0:
                                ev_y = c.sDVE.inc(nc.vector.tensor_copy(yb[i][:, qsl], z_sb[:]))
                            else:
                                ev_y = c.sDVE.inc(nc.vector.tensor_add(yb[i][:, qsl], yb[i][:, qsl], z_sb[:]))
                        c.gset(f"sc_pb{q % 2}", c.sDVE, ev_y)
                        c.gset(f"sc_pc{q % 2}", c.sDVE, ev_y)
                for i, dt in enumerate(dts):
                    nc.sync.wait_ge(c.sDVE.h, c.sDVE.n)
                    ev_sy = c.sX.inc(nc.sync.dma_start(
                        out=scr["y"][dt * 128:(dt + 1) * 128, :], in_=yb[i][:]), 16)
                    c.gset(f"sc_ld{i}", c.sX, ev_sy)

        if not passes[2]:
            return
        c.barrier()
        with contextlib.ExitStack() as esC:
            yr = [esC.enter_context(sbuf(nc, f"mc_yr{i}", [128, TC], BF16)) for i in range(NDT)]
            ty = esC.enter_context(sbuf(nc, "mc_ty", [128, TC], F32))
            tu = esC.enter_context(sbuf(nc, "mc_tu", [128, TC], F32))
            trs = esC.enter_context(sbuf(nc, "mc_rs", [128, TC], BF16))
            xr = esC.enter_context(sbuf(nc, "mc_xr", [128, TC], F32))
            xo = esC.enter_context(sbuf(nc, "mc_xo", [128, TC], F32))
            wpan = [esC.enter_context(sbuf(nc, f"mc_wp{i}", [128, 16 * 128], BF16)) for i in range(2)]
            ps = [esC.enter_context(psum(nc, f"mc_ps{i}", [128, NMM], F32)) for i in range(4)]
            for ch in range(S // TC):
                t0 = ch * TC
                for dt in range(NDT):
                    c.gwait(nc.sync, "mc_ty")
                    c.sX.inc(nc.sync.dma_start(
                        out=ty[:], in_=scr["y"][dt * 128:(dt + 1) * 128, t0:t0 + TC]), 16)
                    c.sX.inc(nc.sync.dma_start(
                        out=tu[:], in_=scr["uD"][dt * 128:(dt + 1) * 128, t0:t0 + TC]), 16)
                    ev_l = c.sX.inc(nc.sync.dma_start(
                        out=trs[:], in_=scr["res"][dt * 128:(dt + 1) * 128, t0:t0 + TC]), 16)
                    nc.vector.wait_ge(c.sX.h, ev_l)
                    c.gwait(nc.vector, f"mc_yr{dt}")
                    nc.vector.tensor_add(ty[:], ty[:], tu[:])
                    ev_yr = c.sDVE.inc(nc.vector.tensor_mul(yr[dt][:], ty[:], trs[:]))
                    c.gset("mc_ty", c.sDVE, ev_yr)
                ev_yr_all = c.sDVE.n
                nc.tensor.wait_ge(c.sDVE.h, ev_yr_all)
                for m in range(8):
                    pb = wpan[m % 2]
                    c.gwait(nc.sync, f"mc_wp{m % 2}")
                    ev_p = c.sW.inc(nc.sync.dma_start(out=pb[:], in_=w["woutp"][m]), 16)
                    c.gwait(nc.sync, "mc_xr")
                    ev_xr = c.sX.inc(nc.sync.dma_start(
                        out=xr[:], in_=x_hbm[m * 128:(m + 1) * 128, t0:t0 + TC]), 16)
                    nc.tensor.wait_ge(c.sW.h, ev_p)
                    for n2 in range(NSUB):
                        slot = (m * NSUB + n2) % 4
                        c.gwait(nc.tensor, f"mc_ps{slot}")
                        for k in range(NDT):
                            mm = nc.tensor.matmul(ps[slot][:], pb[:, k * 128:(k + 1) * 128],
                                                  yr[k][:, n2 * NMM:(n2 + 1) * NMM],
                                                  start=(k == 0), stop=(k == NDT - 1))
                        ev_mm = pe_mark(c, mm)
                        nc.vector.wait_ge(c.sPE.h, ev_mm)
                        nc.vector.wait_ge(c.sX.h, ev_xr)
                        c.gwait(nc.vector, "mc_xo")
                        ev_v = c.sDVE.inc(nc.vector.tensor_add(
                            xo[:, n2 * NMM:(n2 + 1) * NMM], ps[slot][:],
                            xr[:, n2 * NMM:(n2 + 1) * NMM]))
                        c.gset(f"mc_ps{slot}", c.sDVE, ev_v)
                    c.gset("mc_xr", c.sDVE, c.sDVE.n)
                    nc.sync.wait_ge(c.sDVE.h, c.sDVE.n)
                    ev_st = c.sX.inc(nc.sync.dma_start(
                        out=xout_hbm[m * 128:(m + 1) * 128, t0:t0 + TC], in_=xo[:]), 16)
                    c.gset("mc_xo", c.sX, ev_st)
                    c.gset(f"mc_wp{m % 2}", c.sPE, c.sPE.n)
                for dt in range(NDT):
                    c.gset(f"mc_yr{dt}", c.sPE, c.sPE.n)
        c.barrier()


def prep_mamba_host(inp):
    wx = inp["L0_attn_W_x"]
    return {
        "mb_winp": _panels(inp["L0_attn_W_in"], 1024, 4096),
        "mb_wx": np.ascontiguousarray(wx.reshape(16, 128, 96).transpose(1, 0, 2).reshape(128, 16 * 96)),
        "mb_wdt": np.ascontiguousarray(inp["L0_attn_W_dt"]),
        "mb_cw": np.ascontiguousarray(inp["L0_attn_conv_w"].reshape(16, 128, 4).transpose(1, 0, 2).reshape(128, 64)),
        "mb_cb": _cols(inp["L0_attn_conv_b"]),
        "mb_bdt": _cols(inp["L0_attn_b_dt"]),
        "mb_dcol": _cols(inp["L0_attn_D"]),
        "mb_acols": np.ascontiguousarray(inp["L0_attn_A_log"].reshape(16, 128, 16).transpose(1, 0, 2).reshape(128, 256)),
        "mb_woutp": _panels(inp["L0_attn_W_out"], 2048, 1024),
    }


def decl_mamba(nc, cl):
    winp = _wtens(nc, "mb_winp", [32, 128, 8 * 128], cl)
    woutp = _wtens(nc, "mb_woutp", [8, 128, 16 * 128], cl)
    return {
        "winp": [winp.ap()[m] for m in range(32)],
        "woutp": [woutp.ap()[m] for m in range(8)],
        "wx": _wtens(nc, "mb_wx", [128, 16 * 96], cl).ap(),
        "wdt": _wtens(nc, "mb_wdt", [64, DI], cl).ap(),
        "cw": nc.dram_tensor("mb_cw", [128, 64], F32, kind="ExternalInput").ap(),
        "cb": nc.dram_tensor("mb_cb", [128, 16], F32, kind="ExternalInput").ap(),
        "bdt": nc.dram_tensor("mb_bdt", [128, 16], F32, kind="ExternalInput").ap(),
        "dcol": nc.dram_tensor("mb_dcol", [128, 16], F32, kind="ExternalInput").ap(),
        "acols": nc.dram_tensor("mb_acols", [128, 256], F32, kind="ExternalInput").ap(),
    }


def decl_mamba_scratch(nc, kind="Internal"):
    return {
        "delta": nc.dram_tensor("mbs_delta", [DI, S], F32, kind=kind).ap(),
        "du": nc.dram_tensor("mbs_du", [DI, S], F32, kind=kind).ap(),
        "uD": nc.dram_tensor("mbs_uD", [DI, S], F32, kind=kind).ap(),
        "res": nc.dram_tensor("mbs_res", [DI, S], BF16, kind=kind).ap(),
        "y": nc.dram_tensor("mbs_y", [DI, S], F32, kind=kind).ap(),
    }


def build_phase_mamba(passes=(1, 1, 1), dbg=False):
    nc = bass.Bass("TRN2", target_bir_lowering=False)
    x_in = nc.dram_tensor("x_in", [D, S], F32, kind="ExternalInput")
    x_out = nc.dram_tensor("x_out", [D, S], F32, kind="ExternalOutput")
    xn_scr = nc.dram_tensor("xn_scr", [D, S], BF16)
    cl = []
    w = decl_mamba(nc, cl)
    scr = decl_mamba_scratch(nc, kind="ExternalOutput" if dbg else "Internal")
    with contextlib.ExitStack() as es:
        c = Ctx(nc, es)
        emit_casts(c, cl)
        emit_ln(c, x_in.ap(), xn_scr.ap())
        emit_mamba(c, w, x_in.ap(), xn_scr.ap(), x_out.ap(), scr, passes=passes)
        nc.sync.wait_ge(c.sX.h, c.sX.n)
        nc.sync.wait_ge(c.sW.h, c.sW.n)
    return nc


def emit_moe(c, w, x_hbm, xn_hbm, xout_hbm, dbg_tokw=None, dbg_route=None, dbg_exp=None):
    nc = c.nc
    with contextlib.ExitStack() as es:
        tokwT = es.enter_context(sbuf(nc, "mo_tokwT", [8, S], F32))
        sel8 = es.enter_context(sbuf(nc, "mo_sel8", [8, 8 * 128], F32))
        idf32 = es.enter_context(sbuf(nc, "mo_idf", [128, 128], F32))
        gatep = es.enter_context(sbuf(nc, "mo_gate", [128, 64], BF16))
        iota8 = es.enter_context(sbuf(nc, "mo_iota", [128, 8], F32))
        nc.gpsimd.memset(sel8[:], 1.0)
        nc.gpsimd.affine_select(sel8[:], sel8[:], [[-1, 8], [0, 128]], ALU.is_equal, 0.0,
                                base=0, channel_multiplier=1)
        nc.gpsimd.memset(idf32[:], 1.0)
        nc.gpsimd.affine_select(idf32[:], idf32[:], [[-1, 128]], ALU.is_equal, 0.0,
                                base=0, channel_multiplier=1)
        ev_c = c.sPL.inc(nc.gpsimd.iota(iota8[:], [[1, 8]], channel_multiplier=0,
                                        allow_small_or_imprecise_dtypes=True))
        ev_g = c.sW.inc(nc.sync.dma_start(out=gatep[:], in_=w["gatep"]), 16)
        nc.tensor.wait_ge(c.sPL.h, ev_c)
        nc.tensor.wait_ge(c.sW.h, ev_g)
        nc.vector.wait_ge(c.sPL.h, ev_c)

        with contextlib.ExitStack() as esR:
            xk = [esR.enter_context(sbuf(nc, f"mo_xk{i}", [128, S], BF16)) for i in range(8)]
            lg = esR.enter_context(sbuf(nc, "mo_lg", [128, 8], F32))
            mx8 = esR.enter_context(sbuf(nc, "mo_mx8", [128, 8], F32))
            ixu = esR.enter_context(sbuf(nc, "mo_ixu", [128, 8], U32))
            ixf = esR.enter_context(sbuf(nc, "mo_ixf", [128, 2], F32))
            e2 = esR.enter_context(sbuf(nc, "mo_e2", [128, 1], F32))
            w1v = esR.enter_context(sbuf(nc, "mo_w1", [128, 1], F32))
            w2v = esR.enter_context(sbuf(nc, "mo_w2", [128, 1], F32))
            oh = esR.enter_context(sbuf(nc, "mo_oh", [128, 8], F32))
            tkw = esR.enter_context(sbuf(nc, "mo_tkw", [128, 8], F32))
            lg_ps = esR.enter_context(psum(nc, "mo_lgps", [128, 8], F32))
            dbg_route_sb = esR.enter_context(sbuf(nc, "mo_dbgr", [128, 28], F32)) if dbg_route is not None else None
            tw_ps = esR.enter_context(psum(nc, "mo_twps", [8, 128], F32))
            ev_x = 0
            for k in range(8):
                ev_x = c.sX.inc(nc.sync.dma_start(
                    out=xk[k][:], in_=xn_hbm[k * 128:(k + 1) * 128, :]), 16)
            nc.tensor.wait_ge(c.sX.h, ev_x)
            ones8 = esR.enter_context(sbuf(nc, "mo_ones8", [128, 8], F32))
            onec = esR.enter_context(sbuf(nc, "mo_onec", [128, 1], F32))
            nc.gpsimd.memset(ones8[:], 1.0)
            ev_o8 = c.sPL.inc(nc.gpsimd.memset(onec[:], 1.0))
            nc.vector.wait_ge(c.sPL.h, ev_o8)
            ixb = esR.enter_context(sbuf(nc, "mo_ixb", [128, 8], F32))
            wbx = esR.enter_context(sbuf(nc, "mo_wbx", [128, 8], F32))

            def V(op, *args):
                ev = c.sDVE.inc(getattr(nc.vector, op)(*args))
                nc.vector.wait_ge(c.sDVE.h, ev)
                return ev

            for tt in range(32):
                c.gwait(nc.tensor, "mo_lgps")
                for k in range(8):
                    mm = nc.tensor.matmul(lg_ps[:], xk[k][:, tt * 128:(tt + 1) * 128],
                                          gatep[:, k * 8:(k + 1) * 8],
                                          start=(k == 0), stop=(k == 7))
                ev_mm = pe_mark(c, mm)
                nc.vector.wait_ge(c.sPE.h, ev_mm)
                c.gwait(nc.vector, "mo_route")
                ev_lg = V("tensor_copy", lg[:], lg_ps[:])
                c.gset("mo_lgps", c.sDVE, ev_lg)
                V("max", mx8[:], lg[:])
                V("max_index", ixu[:], mx8[:], lg[:])
                V("tensor_copy", ixf[:], ixu[:, 0:2])
                ev_d = V("tensor_sub", e2[:], mx8[:, 1:2], mx8[:, 0:1])
                nc.scalar.wait_ge(c.sDVE.h, ev_d)
                ev_e2 = c.sACT.inc(nc.scalar.activation(e2[:], e2[:], AF.Exp))
                nc.vector.wait_ge(c.sACT.h, ev_e2)
                V("tensor_add", w2v[:], e2[:], onec[:])
                V("reciprocal", w1v[:], w2v[:])
                V("tensor_mul", w2v[:], e2[:], w1v[:])
                for j in range(8):
                    V("tensor_copy", ixb[:, j:j + 1], ixf[:, 0:1])
                    V("tensor_copy", wbx[:, j:j + 1], w1v[:, 0:1])
                V("tensor_sub", oh[:], iota8[:], ixb[:])
                V("tensor_mul", oh[:], oh[:], oh[:])
                V("tensor_tensor", oh[:], oh[:], ones8[:], ALU.min)
                V("tensor_sub", oh[:], ones8[:], oh[:])
                V("tensor_mul", tkw[:], oh[:], wbx[:])
                for j in range(8):
                    V("tensor_copy", ixb[:, j:j + 1], ixf[:, 1:2])
                    V("tensor_copy", wbx[:, j:j + 1], w2v[:, 0:1])
                V("tensor_sub", oh[:], iota8[:], ixb[:])
                V("tensor_mul", oh[:], oh[:], oh[:])
                V("tensor_tensor", oh[:], oh[:], ones8[:], ALU.min)
                V("tensor_sub", oh[:], ones8[:], oh[:])
                V("tensor_mul", oh[:], oh[:], wbx[:])
                ev_tkw = V("tensor_add", tkw[:], tkw[:], oh[:])
                c.gset("mo_route", c.sDVE, ev_tkw)
                nc.tensor.wait_ge(c.sDVE.h, ev_tkw)
                c.gwait(nc.tensor, "mo_twps")
                mm = nc.tensor.matmul(tw_ps[:], tkw[:], idf32[:], start=True, stop=True)
                ev_tp = pe_mark(c, mm)
                nc.vector.wait_ge(c.sPE.h, ev_tp)
                ev_tw = c.sDVE.inc(nc.vector.tensor_copy(
                    tokwT[:, tt * 128:(tt + 1) * 128], tw_ps[:]))
                c.gset("mo_twps", c.sDVE, ev_tw)
            c.barrier()

        with contextlib.ExitStack() as esE:
            xnt = [esE.enter_context(sbuf(nc, f"mo_xn{i}", [128, TC], BF16)) for i in range(8)]
            xw = [esE.enter_context(sbuf(nc, f"mo_xw{i}", [128, TC], BF16)) for i in range(8)]
            hid = [esE.enter_context(sbuf(nc, f"mo_h{i}", [128, TC], BF16)) for i in range(32)]
            sg = esE.enter_context(sbuf(nc, "mo_sg", [128, TC], BF16))
            eoacc = [esE.enter_context(sbuf(nc, f"mo_eo{i}", [128, TC], F32)) for i in range(8)]
            xr = esE.enter_context(sbuf(nc, "mo_xr", [128, TC], F32))
            xo = esE.enter_context(sbuf(nc, "mo_xo", [128, TC], F32))
            p13 = [esE.enter_context(sbuf(nc, f"mo_p13{i}", [128, 8 * 128], BF16)) for i in range(4)]
            p2 = [esE.enter_context(sbuf(nc, f"mo_p2{i}", [128, 32 * 128], BF16)) for i in range(2)]
            ps = [esE.enter_context(psum(nc, f"mo_ps{i}", [128, NMM], F32)) for i in range(6)]
            wbc = esE.enter_context(psum(nc, "mo_wbc", [128, TC], F32))
            for ch in range(S // TC):
                t0 = ch * TC
                ev_xn = 0
                for d in range(8):
                    c.gwait(nc.sync, "mo_xnt")
                    ev_xn = c.sX.inc(nc.sync.dma_start(
                        out=xnt[d][:], in_=xn_hbm[d * 128:(d + 1) * 128, t0:t0 + TC]), 16)
                for e in range(NEXP):
                    c.gwait(nc.tensor, "mo_wbc")
                    for n2 in range(NSUB):
                        mm = nc.tensor.matmul(
                            wbc[:, n2 * NMM:(n2 + 1) * NMM],
                            sel8[:, e * 128:(e + 1) * 128],
                            tokwT[:, t0 + n2 * NMM:t0 + (n2 + 1) * NMM],
                            start=True, stop=True)
                    ev_bc = pe_mark(c, mm)
                    nc.vector.wait_ge(c.sPE.h, ev_bc)
                    nc.vector.wait_ge(c.sX.h, ev_xn)
                    for k in range(8):
                        c.gwait(nc.vector, f"mo_xw{k}")
                        ev_xw = c.sDVE.inc(nc.vector.tensor_mul(xw[k][:], xnt[k][:], wbc[:]))
                    c.gset("mo_wbc", c.sDVE, ev_xw)
                    _n_xw = c.sPE.n
                    for m in range(32):
                        pb1 = p13[m % 2]
                        pb3 = p13[2 + m % 2]
                        c.gwait(nc.sync, f"mo_p13{m % 2}")
                        c.sW.inc(nc.sync.dma_start(out=pb1[:], in_=w["w1p"][e][m]), 16)
                        c.gwait(nc.sync, f"mo_p13{2 + m % 2}")
                        ev_p = c.sW.inc(nc.sync.dma_start(out=pb3[:], in_=w["w3p"][e][m]), 16)
                        nc.tensor.wait_ge(c.sW.h, ev_p)
                        nc.tensor.wait_ge(c.sDVE.h, ev_xw)
                        for n2 in range(NSUB):
                            sg_ = (2 * (m * NSUB + n2)) % 4
                            su_ = (2 * (m * NSUB + n2) + 1) % 4
                            c.gwait(nc.tensor, f"mo_ps{sg_}")
                            c.gwait(nc.tensor, f"mo_ps{su_}")
                            for k in range(8):
                                nc.tensor.matmul(ps[sg_][:], pb1[:, k * 128:(k + 1) * 128],
                                                 xnt[k][:, n2 * NMM:(n2 + 1) * NMM],
                                                 start=(k == 0), stop=(k == 7))
                            for k in range(8):
                                mm = nc.tensor.matmul(ps[su_][:], pb3[:, k * 128:(k + 1) * 128],
                                                      xw[k][:, n2 * NMM:(n2 + 1) * NMM],
                                                      start=(k == 0), stop=(k == 7))
                            ev_mm = pe_mark(c, mm)
                            nc.scalar.wait_ge(c.sPE.h, ev_mm)
                            c.gwait(nc.scalar, "mo_sg")
                            ev_a = c.sACT.inc(nc.scalar.activation(
                                sg[:, n2 * NMM:(n2 + 1) * NMM], ps[sg_][:], AF.Silu))
                            nc.vector.wait_ge(c.sACT.h, ev_a)
                            c.gwait(nc.vector, f"mo_hid{m}") if n2 == 0 else None
                            ev_v = c.sDVE.inc(nc.vector.tensor_mul(
                                hid[m][:, n2 * NMM:(n2 + 1) * NMM], ps[su_][:],
                                sg[:, n2 * NMM:(n2 + 1) * NMM]))
                            c.gset(f"mo_ps{sg_}", c.sACT, ev_a)
                            c.gset(f"mo_ps{su_}", c.sDVE, ev_v)
                            c.gset("mo_sg", c.sDVE, ev_v)
                        c.gset(f"mo_p13{m % 2}", c.sPE, c.sPE.n)
                        c.gset(f"mo_p13{2 + m % 2}", c.sPE, c.sPE.n)
                    for k in range(8):
                        c.gset(f"mo_xw{k}", c.sPE, c.sPE.n)
                    ev_hid_done = (c.sDVE.h, c.sDVE.n)
                    for m in range(8):
                        pb = p2[m % 2]
                        c.gwait(nc.sync, f"mo_p2{m % 2}")
                        ev_p = c.sW.inc(nc.sync.dma_start(out=pb[:], in_=w["w2p"][e][m]), 16)
                        nc.tensor.wait_ge(c.sW.h, ev_p)
                        nc.tensor.wait_ge(ev_hid_done[0], ev_hid_done[1])
                        for n2 in range(NSUB):
                            slot = 4 + (m * NSUB + n2) % 2
                            c.gwait(nc.tensor, f"mo_ps{slot}")
                            for k in range(32):
                                mm = nc.tensor.matmul(ps[slot][:], pb[:, k * 128:(k + 1) * 128],
                                                      hid[k][:, n2 * NMM:(n2 + 1) * NMM],
                                                      start=(k == 0), stop=(k == 31))
                            ev_mm = pe_mark(c, mm)
                            nc.vector.wait_ge(c.sPE.h, ev_mm)
                            sl = slice(n2 * NMM, (n2 + 1) * NMM)
                            if e == 0:
                                ev_v = c.sDVE.inc(nc.vector.tensor_copy(eoacc[m][:, sl], ps[slot][:]))
                            else:
                                ev_v = c.sDVE.inc(nc.vector.tensor_add(
                                    eoacc[m][:, sl], eoacc[m][:, sl], ps[slot][:]))
                            c.gset(f"mo_ps{slot}", c.sDVE, ev_v)
                        c.gset(f"mo_p2{m % 2}", c.sPE, c.sPE.n)
                    for m in range(32):
                        c.gset(f"mo_hid{m}", c.sPE, c.sPE.n)
                if dbg_exp is not None and ch == 0:
                    nc.gpsimd.wait_ge(c.sDVE.h, c.sDVE.n)
                    c.sW.inc(nc.sync.dma_start(out=dbg_exp[0:128, 0:TC], in_=xw[0][:]), 16)
                    c.sW.inc(nc.sync.dma_start(out=dbg_exp[0:128, TC:2 * TC], in_=hid[0][:]), 16)
                    c.sW.inc(nc.sync.dma_start(out=dbg_exp[0:128, 2 * TC:3 * TC], in_=eoacc[0][:]), 16)
                for m in range(8):
                    c.gwait(nc.sync, "mo_xr")
                    ev_xr = c.sX.inc(nc.sync.dma_start(
                        out=xr[:], in_=x_hbm[m * 128:(m + 1) * 128, t0:t0 + TC]), 16)
                    nc.vector.wait_ge(c.sX.h, ev_xr)
                    c.gwait(nc.vector, "mo_xo")
                    ev_v = c.sDVE.inc(nc.vector.scalar_tensor_tensor(
                        xo[:], eoacc[m][:], 0.5, xr[:], ALU.mult, ALU.add))
                    c.gset("mo_xr", c.sDVE, ev_v)
                    nc.sync.wait_ge(c.sDVE.h, ev_v)
                    ev_st = c.sX.inc(nc.sync.dma_start(
                        out=xout_hbm[m * 128:(m + 1) * 128, t0:t0 + TC], in_=xo[:]), 16)
                    c.gset("mo_xo", c.sX, ev_st)
                c.gset("mo_xnt", c.sPE, c.sPE.n)
                c.barrier()
        c.barrier()


def prep_moe_host(inp, prefix="L1_ffn"):
    w1 = inp[f"{prefix}_experts_w1"]
    w2 = inp[f"{prefix}_experts_w2"]
    w3 = inp[f"{prefix}_experts_w3"]
    return {
        "moe_w1p": np.stack([_panels(w1[e], 1024, 4096) for e in range(NEXP)]),
        "moe_w3p": np.stack([_panels(w3[e], 1024, 4096) for e in range(NEXP)]),
        "moe_w2p": np.stack([_panels(w2[e], 4096, 1024) for e in range(NEXP)]),
        "moe_gatep": np.ascontiguousarray(
            inp[f"{prefix}_gate"].reshape(8, 128, 8).transpose(1, 0, 2).reshape(128, 64)),
    }


def decl_moe(nc, cl):
    w1p = _wtens(nc, "moe_w1p", [NEXP, 32, 128, 8 * 128], cl)
    w3p = _wtens(nc, "moe_w3p", [NEXP, 32, 128, 8 * 128], cl)
    w2p = _wtens(nc, "moe_w2p", [NEXP, 8, 128, 32 * 128], cl)
    return {
        "w1p": [[w1p.ap()[e, m] for m in range(32)] for e in range(NEXP)],
        "w3p": [[w3p.ap()[e, m] for m in range(32)] for e in range(NEXP)],
        "w2p": [[w2p.ap()[e, m] for m in range(8)] for e in range(NEXP)],
        "gatep": _wtens(nc, "moe_gatep", [128, 64], cl).ap(),
    }


def build_phase_moe(dbg=False):
    nc = bass.Bass("TRN2", target_bir_lowering=False)
    x_in = nc.dram_tensor("x_in", [D, S], F32, kind="ExternalInput")
    x_out = nc.dram_tensor("x_out", [D, S], F32, kind="ExternalOutput")
    xn_scr = nc.dram_tensor("xn_scr", [D, S], BF16)
    dbg_tokw = nc.dram_tensor("dbg_tokw", [8, S], F32, kind="ExternalOutput").ap() if dbg else None
    dbg_route = nc.dram_tensor("dbg_route", [128, 28], F32, kind="ExternalOutput").ap() if dbg else None
    dbg_exp = nc.dram_tensor("dbg_exp", [128, 3 * TC], F32, kind="ExternalOutput").ap() if dbg else None
    cl = []
    w = decl_moe(nc, cl)
    with contextlib.ExitStack() as es:
        c = Ctx(nc, es)
        emit_casts(c, cl)
        emit_ln(c, x_in.ap(), xn_scr.ap())
        emit_moe(c, w, x_in.ap(), xn_scr.ap(), x_out.ap(), dbg_tokw=dbg_tokw, dbg_route=dbg_route, dbg_exp=dbg_exp)
        nc.sync.wait_ge(c.sX.h, c.sX.n)
        nc.sync.wait_ge(c.sW.h, c.sW.n)
    return nc


def emit_attn(c, w, x_hbm, xn_hbm, xout_hbm, scr):
    nc = c.nc
    IC = 512
    with contextlib.ExitStack() as es:
        bq = es.enter_context(sbuf(nc, "at_bq", [128, 8], F32))
        bl = es.enter_context(sbuf(nc, "at_bl", [128, 2], F32))
        bk = es.enter_context(sbuf(nc, "at_bk", [128, 8], F32))
        bv = es.enter_context(sbuf(nc, "at_bv", [128, 8], F32))
        bo = es.enter_context(sbuf(nc, "at_bo", [128, 8], F32))
        ones2 = es.enter_context(sbuf(nc, "at_ones2", [128, 2], BF16))
        nmx = es.enter_context(sbuf(nc, "at_nmx", [2, 16], F32))
        qmx = es.enter_context(sbuf(nc, "at_qmx", [2, 16], F32))
        negMb = es.enter_context(sbuf(nc, "at_negM", [128, 16], F32))
        kT = [es.enter_context(sbuf(nc, f"at_k{i}", [128, S], BF16)) for i in range(8)]
        vsb = [es.enter_context(sbuf(nc, f"at_v{i}", [128, 16 * 65], BF16)) for i in range(32)]
        ev_b = 0
        for dst, srcn in ((bq, "bq"), (bl, "bl"), (bk, "bk"), (bv, "bv"), (bo, "bo")):
            ev_b = c.sX.inc(nc.sync.dma_start(out=dst[:], in_=w[srcn]), 16)
        nc.scalar.wait_ge(c.sX.h, ev_b)
        nc.gpsimd.memset(ones2[:], 0.0)
        nc.gpsimd.memset(ones2[0:64, 0:1], 1.0)
        ev_o2 = c.sPL.inc(nc.gpsimd.memset(ones2[64:128, 1:2], 1.0))
        nc.tensor.wait_ge(c.sPL.h, ev_o2)
        nc.vector.memset(qmx[:], 0.0)
        ev_qm = c.sDVE.inc(nc.vector.memset(nmx[:], 0.0))
        nc.vector.wait_ge(c.sDVE.h, ev_qm)
        for t in range(32):
            nc.gpsimd.memset(vsb[t][:, 64::65], 1.0)
        ev_vs1 = c.sPL.inc(nc.gpsimd.memset(vsb[31][:, 64:65], 1.0))

        with contextlib.ExitStack() as esP:
            xnt = [esP.enter_context(sbuf(nc, f"at_xn{i}", [128, TC], BF16)) for i in range(8)]
            latt = [esP.enter_context(sbuf(nc, f"at_lt{i}", [128, TC], BF16)) for i in range(2)]
            qbf = esP.enter_context(sbuf(nc, "at_qbf", [128, TC], BF16))
            vbf = esP.enter_context(sbuf(nc, "at_vbf", [128, TC], BF16))
            qsq = esP.enter_context(sbuf(nc, "at_qsq", [128, TC], BF16))
            nrm = esP.enter_context(sbuf(nc, "at_nrm", [2, 1], F32))
            wpan = [esP.enter_context(sbuf(nc, f"at_wp{i}", [128, 8 * 128], BF16)) for i in range(2)]
            ps = [esP.enter_context(psum(nc, f"at_ps{i}", [128, NMM], F32)) for i in range(4)]
            psn = esP.enter_context(psum(nc, "at_psn", [2, NMM], F32))
            pst = esP.enter_context(psum(nc, "at_pst", [128, 128], BF16))
            for ch in range(S // TC):
                t0 = ch * TC
                ev_xn = 0
                for d in range(8):
                    c.gwait(nc.sync, "at_xnt")
                    ev_xn = c.sX.inc(nc.sync.dma_start(
                        out=xnt[d][:], in_=xn_hbm[d * 128:(d + 1) * 128, t0:t0 + TC]), 16)

                def proj(panels, n_m, n_k, rhs_bufs, ev_rhs, rhs_sem, bias_t, evac):
                    for m in range(n_m):
                        pb = wpan[m % 2]
                        c.gwait(nc.sync, f"at_wp{m % 2}")
                        ev_p = c.sW.inc(nc.sync.dma_start(out=pb[:, 0:n_k * 128], in_=panels[m]), 16)
                        nc.tensor.wait_ge(c.sW.h, ev_p)
                        nc.tensor.wait_ge(rhs_sem, ev_rhs)
                        for n2 in range(NSUB):
                            slot = (m * NSUB + n2) % 4
                            c.gwait(nc.tensor, f"at_ps{slot}")
                            for k in range(n_k):
                                mm = nc.tensor.matmul(ps[slot][:], pb[:, k * 128:(k + 1) * 128],
                                                      rhs_bufs[k][:, n2 * NMM:(n2 + 1) * NMM],
                                                      start=(k == 0), stop=(k == n_k - 1))
                            ev_mm = pe_mark(c, mm)
                            evac(m, n2, slot, ev_mm, bias_t)
                        c.gset(f"at_wp{m % 2}", c.sPE, c.sPE.n)

                def evac_q(m, n2, slot, ev_mm, bias_t):
                    sl = slice(n2 * NMM, (n2 + 1) * NMM)
                    nc.scalar.wait_ge(c.sPE.h, ev_mm)
                    c.gwait(nc.scalar, "at_qbf")
                    ev_a = c.sACT.inc(nc.scalar.activation(
                        qbf[:, sl], ps[slot][:], AF.Identity, bias=bias_t[:, m:m + 1]))
                    ev_sq = c.sACT.inc(nc.scalar.activation(qsq[:, sl], qbf[:, sl], AF.Square))
                    c.gset(f"at_ps{slot}", c.sACT, ev_a)
                    nc.sync.wait_ge(c.sACT.h, ev_a)
                    ev_st = c.sX.inc(nc.sync.dma_start(
                        out=scr["qT"][m * 128:(m + 1) * 128, t0 + n2 * NMM:t0 + (n2 + 1) * NMM],
                        in_=qbf[:, sl]), 16)
                    c.gset("at_qbf", c.sX, ev_st)
                    nc.tensor.wait_ge(c.sACT.h, ev_sq)
                    c.gwait(nc.tensor, "at_psn")
                    mm2 = nc.tensor.matmul(psn[:], ones2[:], qsq[:, sl], start=True, stop=True)
                    ev_n = pe_mark(c, mm2)
                    nc.vector.wait_ge(c.sPE.h, ev_n)
                    c.gwait(nc.vector, "at_nrm")
                    e1 = c.sDVE.inc(nc.vector.reduce_max(nrm[:], psn[:], axis=AX.X))
                    nc.vector.wait_ge(c.sDVE.h, e1)
                    c.gset("at_psn", c.sDVE, e1)
                    e2_ = c.sDVE.inc(nc.vector.tensor_max(
                        qmx[:, m:m + 1], qmx[:, m:m + 1], nrm[:]))
                    nc.vector.wait_ge(c.sDVE.h, e2_)
                    c.gset("at_nrm", c.sDVE, e2_)

                proj(w["wqp"], 8, 8, xnt, ev_xn, c.sX.h, bq, evac_q)

                def evac_l(m, n2, slot, ev_mm, bias_t):
                    sl = slice(n2 * NMM, (n2 + 1) * NMM)
                    nc.scalar.wait_ge(c.sPE.h, ev_mm)
                    c.gwait(nc.scalar, f"at_lt{m}")
                    ev_a = c.sACT.inc(nc.scalar.activation(
                        latt[m][:, sl], ps[slot][:], AF.Identity, bias=bias_t[:, m:m + 1]))
                    c.gset(f"at_ps{slot}", c.sACT, ev_a)

                proj(w["wlp"], 2, 8, xnt, ev_xn, c.sX.h, bl, evac_l)
                ev_lat = (c.sACT.h, c.sACT.n)
                c.gset("at_xnt", c.sPE, c.sPE.n)

                def evac_k(m, n2, slot, ev_mm, bias_t):
                    sl = slice(t0 + n2 * NMM, t0 + (n2 + 1) * NMM)
                    sl2 = slice(n2 * NMM, (n2 + 1) * NMM)
                    nc.scalar.wait_ge(c.sPE.h, ev_mm)
                    ev_a = c.sACT.inc(nc.scalar.activation(
                        kT[m][:, sl], ps[slot][:], AF.Identity, bias=bias_t[:, m:m + 1]))
                    c.gwait(nc.scalar, "at_qbf")
                    ev_sq = c.sACT.inc(nc.scalar.activation(qsq[:, sl2], kT[m][:, sl], AF.Square))
                    c.gset(f"at_ps{slot}", c.sACT, ev_a)
                    nc.tensor.wait_ge(c.sACT.h, ev_sq)
                    c.gwait(nc.tensor, "at_psn")
                    mm2 = nc.tensor.matmul(psn[:], ones2[:], qsq[:, sl2], start=True, stop=True)
                    ev_n = pe_mark(c, mm2)
                    c.gset("at_qbf", c.sPE, ev_n)
                    nc.vector.wait_ge(c.sPE.h, ev_n)
                    c.gwait(nc.vector, "at_nrm")
                    e1 = c.sDVE.inc(nc.vector.reduce_max(nrm[:], psn[:], axis=AX.X))
                    nc.vector.wait_ge(c.sDVE.h, e1)
                    c.gset("at_psn", c.sDVE, e1)
                    e2_ = c.sDVE.inc(nc.vector.tensor_max(
                        qmx[:, 8 + m:9 + m], qmx[:, 8 + m:9 + m], nrm[:]))
                    nc.vector.wait_ge(c.sDVE.h, e2_)
                    c.gset("at_nrm", c.sDVE, e2_)

                nc.tensor.wait_ge(ev_lat[0], ev_lat[1])
                proj(w["wkp"], 8, 2, latt, ev_lat[1], ev_lat[0], bk, evac_k)

                def evac_v(m, n2, slot, ev_mm, bias_t):
                    sl = slice(n2 * NMM, (n2 + 1) * NMM)
                    nc.scalar.wait_ge(c.sPE.h, ev_mm)
                    if n2 == 0:
                        c.gwait(nc.scalar, "at_vbf")
                    ev_a = c.sACT.inc(nc.scalar.activation(
                        vbf[:, sl], ps[slot][:], AF.Identity, bias=bias_t[:, m:m + 1]))
                    c.gset(f"at_ps{slot}", c.sACT, ev_a)
                    if n2 == NSUB - 1:
                        nc.tensor.wait_ge(c.sACT.h, ev_a)
                        for par in range(2):
                            h = 2 * m + par
                            for jt in range(TC // 128):
                                tile = (t0 // 128) + jt
                                c.gwait(nc.tensor, "at_pst")
                                mmt = nc.tensor.transpose(
                                    pst[0:128, 0:64],
                                    vbf[64 * par:64 * par + 64, jt * 128:(jt + 1) * 128],
                                    c.ident_bf[64 * par:64 * par + 64, 64 * par:64 * par + 64])
                                ev_t = pe_mark(c, mmt)
                                nc.vector.wait_ge(c.sPE.h, ev_t)
                                ev_cv = c.sDVE.inc(nc.vector.tensor_copy(
                                    vsb[tile][:, 65 * h:65 * h + 64], pst[0:128, 0:64]))
                                c.gset("at_pst", c.sDVE, ev_cv)
                        c.gset("at_vbf", c.sPE, c.sPE.n)

                proj(w["wvp"], 8, 2, latt, ev_lat[1], ev_lat[0], bv, evac_v)
                for m in range(2):
                    c.gset(f"at_lt{m}", c.sPE, c.sPE.n)
            c.barrier()

        with contextlib.ExitStack() as esM:
            t28 = esM.enter_context(sbuf(nc, "at_t28", [2, 8], F32))
            r116 = esM.enter_context(sbuf(nc, "at_r116", [1, 16], F32))
            psm = esM.enter_context(psum(nc, "at_psm", [128, 16], F32))
            ev = c.sDVE.inc(nc.vector.tensor_mul(t28[:], qmx[:, 0:8], qmx[:, 8:16]))
            nc.scalar.wait_ge(c.sDVE.h, ev)
            ev = c.sACT.inc(nc.scalar.activation(t28[:], t28[:], AF.Sqrt))
            nc.vector.wait_ge(c.sACT.h, ev)
            ev = c.sDVE.inc(nc.vector.tensor_scalar(t28[:], t28[:], -SCALE_ATTN * 1.0001, None, ALU.mult))
            nc.sync.wait_ge(c.sDVE.h, ev)
            ev = c.sX.inc(nc.sync.dma_start(
                out=scr["m16"].rearrange("m p -> p m"), in_=t28[:],
                allow_slow_non_contiguous=True), 16)
            nc.sync.wait_ge(c.sX.h, ev)
            ev = c.sX.inc(nc.sync.dma_start(
                out=r116[:], in_=scr["m16"].rearrange("m p -> (m p)")[None, :]), 16)
            nc.tensor.wait_ge(c.sX.h, ev)
            mm = nc.tensor.matmul(psm[:], c.ones_f32[0:1, :], r116[:], start=True, stop=True)
            ev = pe_mark(c, mm)
            nc.vector.wait_ge(c.sPE.h, ev)
            ev = c.sDVE.inc(nc.vector.tensor_copy(negMb[:], psm[:]))
            nc.scalar.wait_ge(c.sDVE.h, ev)
            c.barrier()

        with contextlib.ExitStack() as esC:
            qh = esC.enter_context(sbuf(nc, "at_qh", [128, S], BF16))
            ex = [esC.enter_context(sbuf(nc, f"at_ex{i}", [128, IC], BF16)) for i in range(2)]
            recs = esC.enter_context(sbuf(nc, "at_recs", [128, IC], F32))
            recb = esC.enter_context(sbuf(nc, "at_recb", [64, IC], F32))
            ao = esC.enter_context(sbuf(nc, "at_ao", [64, IC], BF16))
            pss = [esC.enter_context(psum(nc, f"at_pss{i}", [128, IC], F32)) for i in range(2)]
            psa = [esC.enter_context(psum(nc, f"at_psa{i}", [65, IC], F32)) for i in range(2)]
            psr = esC.enter_context(psum(nc, "at_psr", [64, IC], F32))
            nc.tensor.wait_ge(c.sPL.h, ev_vs1)
            for h in range(16):
                kt_tile = kT[h // 2]
                kbase = 64 * (h % 2)
                c.gwait(nc.sync, "at_qh")
                ev_q = c.sX.inc(nc.sync.dma_start(
                    out=qh[kbase:kbase + 64, :], in_=scr["qT"][h * 64:(h + 1) * 64, :]), 16)
                for ic in range(S // IC):
                    pa = psa[ic % 2]

                    def qk(jt):
                        c.gwait(nc.tensor, f"at_pss{jt % 2}")
                        nc.tensor.wait_ge(c.sX.h, ev_q)
                        mm = nc.tensor.matmul(
                            pss[jt % 2][:], kt_tile[kbase:kbase + 64, jt * 128:(jt + 1) * 128],
                            qh[kbase:kbase + 64, ic * IC:(ic + 1) * IC], start=True, stop=True)
                        return pe_mark(c, mm)

                    ev_qk = {0: qk(0)}
                    for jt in range(32):
                        if jt + 1 < 32:
                            ev_qk[jt + 1] = qk(jt + 1)
                        nc.scalar.wait_ge(c.sPE.h, ev_qk[jt])
                        c.gwait(nc.scalar, f"at_ex{jt % 2}")
                        ev_e = c.sACT.inc(nc.scalar.activation(
                            ex[jt % 2][:], pss[jt % 2][:], AF.Exp, scale=SCALE_ATTN,
                            bias=negMb[:, h:h + 1]))
                        c.gset(f"at_pss{jt % 2}", c.sACT, ev_e)
                        nc.tensor.wait_ge(c.sACT.h, ev_e)
                        if jt == 0:
                            c.gwait(nc.tensor, f"at_psa{ic % 2}")
                        mm = nc.tensor.matmul(pa[:], vsb[jt][:, 65 * h:65 * h + 65],
                                              ex[jt % 2][:], start=(jt == 0), stop=(jt == 31))
                        ev_av = pe_mark(c, mm)
                        c.gset(f"at_ex{jt % 2}", c.sPE, ev_av)
                    nc.vector.wait_ge(c.sPE.h, ev_av)
                    c.gwait(nc.vector, "at_recs")
                    ev1 = c.sDVE.inc(nc.vector.tensor_copy(recs[64:65, :], pa[64:65, :]))
                    nc.vector.wait_ge(c.sDVE.h, ev1)
                    ev2 = c.sDVE.inc(nc.vector.reciprocal(recs[64:65, :], recs[64:65, :]))
                    nc.vector.wait_ge(c.sDVE.h, ev2)
                    nc.tensor.wait_ge(c.sDVE.h, ev2)
                    c.gwait(nc.tensor, "at_psr")
                    mm = nc.tensor.matmul(psr[:], c.ones_bf[64:65, 0:64] if False else c.ones_f32[64:65, 0:64],
                                          recs[64:65, :], start=True, stop=True)
                    ev3 = pe_mark(c, mm)
                    nc.vector.wait_ge(c.sPE.h, ev3)
                    c.gwait(nc.vector, "at_recb")
                    ev4 = c.sDVE.inc(nc.vector.tensor_copy(recb[:], psr[:]))
                    c.gset("at_psr", c.sDVE, ev4)
                    c.gset("at_recs", c.sDVE, ev4)
                    nc.vector.wait_ge(c.sDVE.h, ev4)
                    c.gwait(nc.vector, "at_ao")
                    ev5 = c.sDVE.inc(nc.vector.tensor_mul(ao[:], pa[0:64, :], recb[:]))
                    c.gset(f"at_psa{ic % 2}", c.sDVE, ev5)
                    c.gset("at_recb", c.sDVE, ev5)
                    nc.sync.wait_ge(c.sDVE.h, ev5)
                    ev_st = c.sX.inc(nc.sync.dma_start(
                        out=scr["aoT"][h * 64:(h + 1) * 64, ic * IC:(ic + 1) * IC],
                        in_=ao[:]), 16)
                    c.gset("at_ao", c.sX, ev_st)
                c.gset("at_qh", c.sPE, c.sPE.n)
            c.barrier()

        with contextlib.ExitStack() as esO:
            aot = [esO.enter_context(sbuf(nc, f"at_ao{i}", [128, TC], BF16)) for i in range(8)]
            xr = esO.enter_context(sbuf(nc, "at_xr", [128, TC], F32))
            xo = esO.enter_context(sbuf(nc, "at_xo", [128, TC], F32))
            tf = esO.enter_context(sbuf(nc, "at_tf", [128, TC], F32))
            wpan = [esO.enter_context(sbuf(nc, f"at_wo{i}", [128, 8 * 128], BF16)) for i in range(2)]
            ps = [esO.enter_context(psum(nc, f"at_po{i}", [128, NMM], F32)) for i in range(4)]
            for ch in range(S // TC):
                t0 = ch * TC
                ev_ao = 0
                for d in range(8):
                    c.gwait(nc.sync, "ao_ld")
                    ev_ao = c.sX.inc(nc.sync.dma_start(
                        out=aot[d][:], in_=scr["aoT"][d * 128:(d + 1) * 128, t0:t0 + TC]), 16)
                for m in range(8):
                    pb = wpan[m % 2]
                    c.gwait(nc.sync, f"at_wo{m % 2}")
                    ev_p = c.sW.inc(nc.sync.dma_start(out=pb[:], in_=w["wop"][m]), 16)
                    c.gwait(nc.sync, "at_xr")
                    ev_xr = c.sX.inc(nc.sync.dma_start(
                        out=xr[:], in_=x_hbm[m * 128:(m + 1) * 128, t0:t0 + TC]), 16)
                    nc.tensor.wait_ge(c.sW.h, ev_p)
                    nc.tensor.wait_ge(c.sX.h, ev_ao)
                    for n2 in range(NSUB):
                        slot = (m * NSUB + n2) % 4
                        c.gwait(nc.tensor, f"at_po{slot}")
                        for k in range(8):
                            mm = nc.tensor.matmul(ps[slot][:], pb[:, k * 128:(k + 1) * 128],
                                                  aot[k][:, n2 * NMM:(n2 + 1) * NMM],
                                                  start=(k == 0), stop=(k == 7))
                        ev_mm = pe_mark(c, mm)
                        sl = slice(n2 * NMM, (n2 + 1) * NMM)
                        nc.scalar.wait_ge(c.sPE.h, ev_mm)
                        c.gwait(nc.scalar, "at_tf")
                        ev_a = c.sACT.inc(nc.scalar.activation(
                            tf[:, sl], ps[slot][:], AF.Identity, bias=bo[:, m:m + 1]))
                        c.gset(f"at_po{slot}", c.sACT, ev_a)
                        nc.vector.wait_ge(c.sACT.h, ev_a)
                        nc.vector.wait_ge(c.sX.h, ev_xr)
                        c.gwait(nc.vector, "at_xo")
                        ev_v = c.sDVE.inc(nc.vector.tensor_add(xo[:, sl], tf[:, sl], xr[:, sl]))
                        c.gset("at_tf", c.sDVE, ev_v)
                    c.gset("at_xr", c.sDVE, c.sDVE.n)
                    nc.sync.wait_ge(c.sDVE.h, c.sDVE.n)
                    ev_st = c.sX.inc(nc.sync.dma_start(
                        out=xout_hbm[m * 128:(m + 1) * 128, t0:t0 + TC], in_=xo[:]), 16)
                    c.gset("at_xo", c.sX, ev_st)
                    c.gset(f"at_wo{m % 2}", c.sPE, c.sPE.n)
                c.gset("ao_ld", c.sPE, c.sPE.n)
            c.barrier()
        c.barrier()


def prep_attn_host(inp):
    return {
        "at_wqp": _panels(inp["L1_attn_Wq"], 1024, 1024),
        "at_wlp": _panels(inp["L1_attn_Wl"], 1024, 256),
        "at_wkp": _panels(inp["L1_attn_Wk"], 256, 1024),
        "at_wvp": _panels(inp["L1_attn_Wv"], 256, 1024),
        "at_wop": _panels(inp["L1_attn_Wo"], 1024, 1024),
        "atb_bq": _cols(inp["L1_attn_bq"]),
        "atb_bl": _cols(inp["L1_attn_bl"]),
        "atb_bk": _cols(inp["L1_attn_bk"]),
        "atb_bv": _cols(inp["L1_attn_bv"]),
        "atb_bo": _cols(inp["L1_attn_bo"]),
    }


def decl_attn(nc, cl):
    wqp = _wtens(nc, "at_wqp", [8, 128, 8 * 128], cl)
    wlp = _wtens(nc, "at_wlp", [2, 128, 8 * 128], cl)
    wkp = _wtens(nc, "at_wkp", [8, 128, 2 * 128], cl)
    wvp = _wtens(nc, "at_wvp", [8, 128, 2 * 128], cl)
    wop = _wtens(nc, "at_wop", [8, 128, 8 * 128], cl)
    return {
        "wqp": [wqp.ap()[m] for m in range(8)],
        "wlp": [wlp.ap()[m] for m in range(2)],
        "wkp": [wkp.ap()[m] for m in range(8)],
        "wvp": [wvp.ap()[m] for m in range(8)],
        "wop": [wop.ap()[m] for m in range(8)],
        "bq": nc.dram_tensor("atb_bq", [128, 8], F32, kind="ExternalInput").ap(),
        "bl": nc.dram_tensor("atb_bl", [128, 2], F32, kind="ExternalInput").ap(),
        "bk": nc.dram_tensor("atb_bk", [128, 8], F32, kind="ExternalInput").ap(),
        "bv": nc.dram_tensor("atb_bv", [128, 8], F32, kind="ExternalInput").ap(),
        "bo": nc.dram_tensor("atb_bo", [128, 8], F32, kind="ExternalInput").ap(),
    }


def decl_attn_scratch(nc):
    return {
        "qT": nc.dram_tensor("ats_qT", [D, S], BF16).ap(),
        "aoT": nc.dram_tensor("ats_aoT", [D, S], BF16).ap(),
        "m16": nc.dram_tensor("ats_m16", [8, 2], F32).ap(),
    }


def build_phase_attn():
    nc = bass.Bass("TRN2", target_bir_lowering=False)
    x_in = nc.dram_tensor("x_in", [D, S], F32, kind="ExternalInput")
    x_out = nc.dram_tensor("x_out", [D, S], F32, kind="ExternalOutput")
    xn_scr = nc.dram_tensor("xn_scr", [D, S], BF16)
    cl = []
    w = decl_attn(nc, cl)
    scr = decl_attn_scratch(nc)
    with contextlib.ExitStack() as es:
        c = Ctx(nc, es)
        emit_casts(c, cl)
        emit_ln(c, x_in.ap(), xn_scr.ap())
        emit_attn(c, w, x_in.ap(), xn_scr.ap(), x_out.ap(), scr)
        nc.sync.wait_ge(c.sX.h, c.sX.n)
        nc.sync.wait_ge(c.sW.h, c.sW.n)
    return nc


def _panels(wT, kdim, mdim):
    kt, mt = kdim // 128, mdim // 128
    a = wT.reshape(kt, 128, mt, 128).transpose(2, 1, 0, 3)
    return np.ascontiguousarray(a.reshape(mt, 128, kt * 128))


def _cols(v):
    n = v.shape[0] // 128
    return np.ascontiguousarray(v.reshape(n, 128).T)


def prep_ffn0_host(inp):
    return {
        "ffn_w1p": _panels(np.ascontiguousarray(inp["L0_ffn_w1_w"].T), 1024, 4096),
        "ffn_w2p": _panels(np.ascontiguousarray(inp["L0_ffn_w2_w"].T), 1024, 4096),
        "ffn_h1p": _panels(np.ascontiguousarray(inp["L0_ffn_h1_w"].T), 4096, 4096),
        "ffn_h2p": _panels(np.ascontiguousarray(inp["L0_ffn_h2_w"].T), 4096, 4096),
        "ffn_w3p": _panels(np.ascontiguousarray(inp["L0_ffn_w3_w"].T), 4096, 1024),
        "ffn_s1": _cols(inp["L0_ffn_w1_s"]),
        "ffn_s2": _cols(inp["L0_ffn_w2_s"]),
        "ffn_h1s": _cols(inp["L0_ffn_h1_s"]),
        "ffn_h2s": _cols(inp["L0_ffn_h2_s"]),
        "ffn_w3s": _cols(inp["L0_ffn_w3_s"]),
    }


def _wtens(nc, name, shape, cast_list):
    f = nc.dram_tensor(name, shape, F32, kind="ExternalInput")
    b = nc.dram_tensor(name + "_bf", shape, BF16)
    cast_list.append((b, f))
    return b


def decl_ffn0(nc, cl):
    d = {}
    d["w1p_t"] = _wtens(nc, "ffn_w1p", [32, 128, 8 * 128], cl)
    d["w2p_t"] = _wtens(nc, "ffn_w2p", [32, 128, 8 * 128], cl)
    d["h1p_t"] = _wtens(nc, "ffn_h1p", [32, 128, 32 * 128], cl)
    d["h2p_t"] = _wtens(nc, "ffn_h2p", [32, 128, 32 * 128], cl)
    d["w3p_t"] = _wtens(nc, "ffn_w3p", [8, 128, 32 * 128], cl)
    return {
        "w1p": [d["w1p_t"].ap()[m] for m in range(32)],
        "w2p": [d["w2p_t"].ap()[m] for m in range(32)],
        "h1p": [d["h1p_t"].ap()[m] for m in range(32)],
        "h2p": [d["h2p_t"].ap()[m] for m in range(32)],
        "w3p": [d["w3p_t"].ap()[m] for m in range(8)],
        "s1": nc.dram_tensor("ffn_s1", [128, 32], F32, kind="ExternalInput").ap(),
        "s2": nc.dram_tensor("ffn_s2", [128, 32], F32, kind="ExternalInput").ap(),
        "h1s": nc.dram_tensor("ffn_h1s", [128, 32], F32, kind="ExternalInput").ap(),
        "h2s": nc.dram_tensor("ffn_h2s", [128, 32], F32, kind="ExternalInput").ap(),
        "w3s": nc.dram_tensor("ffn_w3s", [128, 8], F32, kind="ExternalInput").ap(),
    }


def build_ln_only(dbg=False):
    nc = bass.Bass("TRN2", target_bir_lowering=False)
    x_in = nc.dram_tensor("x_in", [D, S], F32, kind="ExternalInput")
    xn_out = nc.dram_tensor("xn_out", [D, S], BF16, kind="ExternalOutput")
    dbg_stats = nc.dram_tensor("dbg_stats", [4, S], F32, kind="ExternalOutput").ap() if dbg else None
    with contextlib.ExitStack() as es:
        c = Ctx(nc, es)
        emit_ln(c, x_in.ap(), xn_out.ap(), dbg_stats=dbg_stats)
        nc.sync.wait_ge(c.sX.h, c.sX.n)
    return nc


def build_phase_ffn0():
    nc = bass.Bass("TRN2", target_bir_lowering=False)
    x_in = nc.dram_tensor("x_in", [D, S], F32, kind="ExternalInput")
    x_out = nc.dram_tensor("x_out", [D, S], F32, kind="ExternalOutput")
    xn_scr = nc.dram_tensor("xn_scr", [D, S], BF16)
    cl = []
    w = decl_ffn0(nc, cl)
    with contextlib.ExitStack() as es:
        c = Ctx(nc, es)
        emit_casts(c, cl)
        emit_ln(c, x_in.ap(), xn_scr.ap())
        emit_ffn0(c, w, x_in.ap(), xn_scr.ap(), x_out.ap())
        nc.sync.wait_ge(c.sX.h, c.sX.n)
        nc.sync.wait_ge(c.sW.h, c.sW.n)
    return nc


def emit_casts(c, cl):
    nc = c.nc
    for b, f in cl:
        ev = c.sPL.inc(nc.gpsimd.dma_start(out=b.ap(), in_=f.ap()), 16)
    nc.sync.wait_ge(c.sPL.h, ev)
    return ev


def build_fused(dbg=False):
    nc = bass.Bass("TRN2", target_bir_lowering=False)
    kd = "ExternalOutput" if dbg else "Internal"
    x0 = nc.dram_tensor("x_in", [D, S], F32, kind="ExternalInput")
    out = nc.dram_tensor("x_out", [D, S], F32, kind="ExternalOutput")
    xa = nc.dram_tensor("x_scr_a", [D, S], F32, kind=kd)
    xb = nc.dram_tensor("x_scr_b", [D, S], F32, kind=kd)
    xn = nc.dram_tensor("xn_scr", [D, S], BF16)
    cl = []
    w_ffn = decl_ffn0(nc, cl)
    w_mb = decl_mamba(nc, cl)
    w_moe = decl_moe(nc, cl)
    w_at = decl_attn(nc, cl)
    scr_mb = decl_mamba_scratch(nc)
    scr_at = decl_attn_scratch(nc)
    with contextlib.ExitStack() as es:
        c = Ctx(nc, es)
        emit_casts(c, cl)
        emit_ln(c, x0.ap(), xn.ap())
        emit_ffn0(c, w_ffn, x0.ap(), xn.ap(), xa.ap())
        emit_ln(c, xa.ap(), xn.ap())
        emit_mamba(c, w_mb, xa.ap(), xn.ap(), xb.ap(), scr_mb)
        emit_ln(c, xb.ap(), xn.ap())
        emit_ffn0(c, w_ffn, xb.ap(), xn.ap(), xa.ap())
        emit_ln(c, xa.ap(), xn.ap())
        emit_moe(c, w_moe, xa.ap(), xn.ap(), xb.ap())
        emit_ln(c, xb.ap(), xn.ap())
        emit_attn(c, w_at, xb.ap(), xn.ap(), xa.ap(), scr_at)
        emit_ln(c, xa.ap(), xn.ap())
        emit_moe(c, w_moe, xa.ap(), xn.ap(), out.ap())
        nc.sync.wait_ge(c.sX.h, c.sX.n)
        nc.sync.wait_ge(c.sW.h, c.sW.n)
    return nc


def kernel_fused(dbg=False, **inputs):
    import time as _t
    t0 = _t.time()
    nc = build_fused(dbg)
    w = {}
    w.update(prep_ffn0_host(inputs))
    w.update(prep_mamba_host(inputs))
    w.update(prep_moe_host(inputs))
    w.update(prep_attn_host(inputs))
    maps = []
    for cid in range(8):
        m = dict(w)
        m["x_in"] = np.ascontiguousarray(inputs["L0_attn_x"][cid].T)
        maps.append(m)
    res = run_bass_kernel_spmd(nc, maps, core_ids=list(range(8)))
    if dbg:
        return res
    return np.stack([res.results[cid]["x_out"].T for cid in range(8)]).astype(np.float32)


_PROG_CACHE = {}


def _phase_progs():
    if "ffn0" not in _PROG_CACHE:
        _PROG_CACHE["ffn0"] = build_phase_ffn0()
        _PROG_CACHE["mamba"] = build_phase_mamba()
        _PROG_CACHE["moe"] = build_phase_moe()
        _PROG_CACHE["attn"] = build_phase_attn()
    return _PROG_CACHE


def kernel(**inputs):
    return kernel_fused(**inputs)


def kernel_unfused(**inputs):
    progs = _phase_progs()
    w_ffn = prep_ffn0_host(inputs)
    w_mb = prep_mamba_host(inputs)
    w_moe = prep_moe_host(inputs)
    w_at = prep_attn_host(inputs)
    x = [np.ascontiguousarray(inputs["L0_attn_x"][cid].T) for cid in range(8)]

    def launch(prog, wmap):
        maps = []
        for cid in range(8):
            m = dict(wmap)
            m["x_in"] = x[cid]
            maps.append(m)
        res = run_bass_kernel_spmd(progs[prog], maps, core_ids=list(range(8)))
        for cid in range(8):
            x[cid] = np.ascontiguousarray(res.results[cid]["x_out"])

    launch("ffn0", w_ffn)
    launch("mamba", w_mb)
    launch("ffn0", w_ffn)
    launch("moe", w_moe)
    launch("attn", w_at)
    launch("moe", w_moe)
    return np.stack([x[cid].T for cid in range(8)]).astype(np.float32)
